# version 42
# baseline (speedup 1.0000x reference)
import contextlib
import numpy as np

import concourse.bass as bass
import concourse.mybir as mybir
from concourse.bass_utils import run_bass_kernel_spmd

F32 = mybir.dt.float32
BF16 = mybir.dt.bfloat16
AF = mybir.ActivationFunctionType
ALU = mybir.AluOpType
AX = mybir.AxisListType

TOK = 1024
H = 2048
NHC = 16
NH = 16
NKV = 4
INTER = 5632
NIC = 44
SC = 0.08838834764831845
EPS = 1e-5

NBIG = 90624
R0 = 0
XT_OWN = 16384
WK = 32768
WV = 40960
AO = 32768
KS = 49152
VS = 57344
QS = 65536
PS0 = 81920
ONES = 82944
XNT = 84480
DS = 88576
GUW = (28672, 30720, 84480, 86528)
XT_P4 = 65536
DWA = 65536
DWB_PIECES = ((76800, 0, 20), (84480, 20, 16), (28672, 36, 8))


def hdn_base(ic):
    if ic < 16:
        return ic * 1024
    if ic < 32:
        return 49152 + (ic - 16) * 1024
    return 16384 + (ic - 32) * 1024


def dwb_tile(ic):
    for base, lo, n in DWB_PIECES:
        if lo <= ic < lo + n:
            return base + (ic - lo) * 256
    raise AssertionError

SEMS = ["T", "V", "S", "dx0", "dx1", "dtp0", "dtp1", "dwc", "dwv", "dwk",
        "dwq0", "dwq1", "dwq2", "dwq3", "dwo0", "dwo1", "dwo2", "dwo3",
        "dcwo", "dcg", "dcu", "dcd",
        "dgu0", "dgu1", "dgu2", "dgu3", "ddw0", "ddw1", "dout"]


class P:
    def __init__(self):
        self.cnt = {s: 0 for s in SEMS}
        self.ops = {e: [] for e in "TVSYG"}
        self.lastw = {e: {} for e in "TVSYG"}

    def op(self, eng, fn, waits=(), inc=None, inc16=None):
        w = []
        for item in waits:
            if item is None:
                continue
            s, v = item
            if v is None or v <= 0:
                continue
            if self.lastw[eng].get(s, -1) >= v:
                continue
            w.append((s, v))
            self.lastw[eng][s] = v
        if inc:
            self.cnt[inc] += 1
        if inc16:
            self.cnt[inc16] += 16
        self.ops[eng].append((fn, w, inc, inc16))
        return self.cnt[inc] if inc else (self.cnt[inc16] if inc16 else None)


def build_program(stop_after=None):
    nc = bass.Bass("TRN2", target_bir_lowering=False)
    ap = lambda t: t.ap()
    d_xo = ap(nc.dram_tensor("x_own", [TOK, H], F32, kind="ExternalInput"))
    d_xp = ap(nc.dram_tensor("x_prev", [TOK, H], F32, kind="ExternalInput"))
    d_wq = ap(nc.dram_tensor("wq", [H, H], F32, kind="ExternalInput"))
    d_wk = ap(nc.dram_tensor("wk", [H, 512], F32, kind="ExternalInput"))
    d_wv = ap(nc.dram_tensor("wv", [H, 512], F32, kind="ExternalInput"))
    d_wo = ap(nc.dram_tensor("wo", [H, H], F32, kind="ExternalInput"))
    d_gw = ap(nc.dram_tensor("gate_w", [INTER, H], F32, kind="ExternalInput"))
    d_uw = ap(nc.dram_tensor("up_w", [INTER, H], F32, kind="ExternalInput"))
    d_dw = ap(nc.dram_tensor("down_w", [H, INTER], F32, kind="ExternalInput"))
    d_gs = ap(nc.dram_tensor("gate_s", [128, NIC], F32, kind="ExternalInput"))
    d_us = ap(nc.dram_tensor("up_s", [128, NIC], F32, kind="ExternalInput"))
    d_ds = ap(nc.dram_tensor("down_s_rep", [128, H], F32, kind="ExternalInput"))
    d_msk = ap(nc.dram_tensor("masks", [128, 256], F32, kind="ExternalInput"))
    d_ph = ap(nc.dram_tensor("phantom", [128, 1024], F32, kind="ExternalInput"))
    d_ones = ap(nc.dram_tensor("ones_in", [128, 128], F32, kind="ExternalInput"))
    d_out = ap(nc.dram_tensor("out", [TOK, H], F32, kind="ExternalOutput"))
    d_gwb = ap(nc.dram_tensor("gw_bf", [INTER, H], BF16, kind="Internal"))
    d_uwb = ap(nc.dram_tensor("uw_bf", [INTER, H], BF16, kind="Internal"))
    d_dwb = ap(nc.dram_tensor("dw_bf", [H, INTER], BF16, kind="Internal"))
    d_wob = ap(nc.dram_tensor("wo_bf", [H, 1024], BF16, kind="Internal"))

    p = P()
    with contextlib.ExitStack() as ctx:
        sems = {s: ctx.enter_context(nc.semaphore(s)) for s in SEMS}
        BIG = ctx.enter_context(nc.sbuf_tensor("BIG", [128, NBIG], BF16))
        XB = [ctx.enter_context(nc.sbuf_tensor(f"xb{i}", [128, H], F32)) for i in range(2)]
        FAR = ctx.enter_context(nc.sbuf_tensor("far", [128, 1024], F32))
        ST = ctx.enter_context(nc.sbuf_tensor("stats", [128, 64], F32))
        MSK = ctx.enter_context(nc.sbuf_tensor("msk", [128, 256], F32))
        PH = ctx.enter_context(nc.sbuf_tensor("ph", [128, 1024], F32))
        GSC = ctx.enter_context(nc.sbuf_tensor("gsc", [128, NIC], F32))
        USC = ctx.enter_context(nc.sbuf_tensor("usc", [128, NIC], F32))
        pb = [ctx.enter_context(nc.psum_tensor(f"pb{i}", [128, 512], F32)) for i in range(8)]

        B = lambda a, n: BIG[:, a:a + n]

        bank_free = {i: None for i in range(8)}
        xb_free = [None, None]
        xb_cnt = [0, 0]
        tp_cnt = [0, 0]
        XNS = (XNT, XNT + 2048, 65536, 67584)
        xns_free = {i: None for i in range(4)}

        def load_x(src_ap, rows, slot, ring="Y"):
            xb_cnt[slot] += 1
            p.op(ring, lambda e, s=src_ap, r=rows, sl=slot:
                 e.dma_start(XB[sl][:, :], s[r:r + 128, :]),
                 waits=[xb_free[slot]], inc16=f"dx{slot}")
            return xb_cnt[slot] * 16

        p.op("G", lambda e: e.dma_start(B(ONES, 128), d_ones), inc16="dwc")
        p.op("G", lambda e: e.dma_start(B(DS, H), d_ds), inc16="dwc")
        p.op("Y", lambda e: e.dma_start(MSK[:, :], d_msk), inc16="dwc")
        p.op("Y", lambda e: e.dma_start(PH[:, :], d_ph), inc16="dwc")
        p.op("Y", lambda e: e.dma_start(GSC[:, :], d_gs), inc16="dwc")
        p.op("Y", lambda e: e.dma_start(USC[:, :], d_us), inc16="dwc")
        ev_dwc = p.cnt["dwc"]
        ev_wvk = {}
        for dst, src2, sem in ((WV, d_wv, "dwv"), (WK, d_wk, "dwk")):
            p.op("G", lambda e, d=dst, s=src2: e.dma_start(
                B(d, 8192).rearrange("p (b o) -> p b o", b=16),
                s.rearrange("(b p) o -> p b o", p=128)), inc16=sem)
            ev_wvk[sem] = p.cnt[sem]
        def precast(dst, srcw, rows, sem):
            qr = rows // 4
            for i in range(4):
                p.op("G", lambda e, d=dst, s=srcw, a=i * qr, b2=(i + 1) * qr:
                     e.dma_start(d[a:b2, :], s[a:b2, :]), inc16=sem)
            return p.cnt[sem]

        def ln_pair(specs, extra_s=(), extra_y=(), ring="Y"):
            n = len(specs)
            evx, evsq, xnt, ev = [None] * n, [None] * n, [None] * n, [None] * n
            xns = [sp[5] if len(sp) > 5 else sp[2] for sp in specs]
            for i, sp in enumerate(specs):
                evx[i] = load_x(sp[0], sp[1], sp[2], ring=ring)
            for i, sp in enumerate(specs):
                slot, c = sp[2], sp[2] * 8
                ev[i] = p.op("V", lambda e, sl=slot, c=c: e.reduce_sum(
                    ST[:, c:c + 1], XB[sl][:, :], AX.X),
                    waits=[(f"dx{slot}", evx[i])], inc="V")
            for i, sp in enumerate(specs):
                slot, c = sp[2], sp[2] * 8
                xnt[i] = B(XNS[xns[i]], 2048)
                evsq[i] = p.op("S", lambda e, sl=slot, c=c, xo=xnt[i]: e.activation(
                    xo, XB[sl][:, :], AF.Square, accum_out=ST[:, c + 1:c + 2]),
                    waits=[(f"dx{slot}", evx[i]), xns_free[xns[i]]]
                    + list(extra_s), inc="S")
            def step(i, fn, waits=()):
                slot = specs[i][2]
                ev[i] = p.op("V", fn(slot * 8), waits=[("V", ev[i])] + list(waits), inc="V")
            for i in range(n):
                step(i, lambda c: lambda e: e.tensor_scalar_mul(
                    ST[:, c + 2:c + 3], ST[:, c:c + 1], 1.0 / H))
            for i in range(n):
                step(i, lambda c: lambda e: e.tensor_scalar_mul(
                    ST[:, c + 3:c + 4], ST[:, c + 1:c + 2], 1.0 / H),
                    waits=[("S", evsq[i])])
            for i in range(n):
                step(i, lambda c: lambda e: e.tensor_mul(
                    ST[:, c + 4:c + 5], ST[:, c + 2:c + 3], ST[:, c + 2:c + 3]))
            for i in range(n):
                step(i, lambda c: lambda e: e.scalar_tensor_tensor(
                    ST[:, c + 4:c + 5], ST[:, c + 3:c + 4], EPS, ST[:, c + 4:c + 5],
                    ALU.add, ALU.subtract))
            evsd = [None] * n
            for i in range(n):
                c = specs[i][2] * 8
                evsd[i] = p.op("S", lambda e, c=c: e.activation(
                    ST[:, c + 5:c + 6], ST[:, c + 4:c + 5], AF.Sqrt),
                    waits=[("V", ev[i])], inc="S")
            for i in range(n):
                step(i, lambda c: lambda e: e.reciprocal(
                    ST[:, c + 6:c + 7], ST[:, c + 5:c + 6]), waits=[("S", evsd[i])])
            for i in range(n):
                step(i, lambda c: lambda e: e.scalar_tensor_tensor(
                    ST[:, c + 7:c + 8], ST[:, c + 2:c + 3], -1.0, ST[:, c + 6:c + 7],
                    ALU.mult, ALU.mult))
            for i, sp in enumerate(specs):
                src_ap, rows, slot, dest_base, tcl = sp[:5]
                c = slot * 8
                if i % 2 == 0:
                    ev_ap2 = p.op("V", lambda e, sl=slot, c=c, xo=xnt[i]: e.tensor_scalar(
                        xo, XB[sl][:, :], ST[:, c + 6:c + 7], ST[:, c + 7:c + 8],
                        ALU.mult, ALU.add), waits=[("V", ev[i])], inc="V")
                    xb_free[slot] = ("V", ev_ap2)
                    apw = ("V", ev_ap2)
                else:
                    ev_ap2 = p.op("S", lambda e, sl=slot, c=c, xo=xnt[i]: e.activation(
                        xo, XB[sl][:, :], AF.Identity, bias=ST[:, c + 7:c + 8],
                        scale=ST[:, c + 6:c + 7]), waits=[("V", ev[i])], inc="S")
                    xb_free[slot] = ("S", ev_ap2)
                    apw = ("S", ev_ap2)
                tp_cnt[slot] += 1
                p.op("Y", lambda e, xo=xnt[i], db=dest_base, t=tcl: e.dma_start(
                    B(db, 16384).rearrange("p (b t) -> p b t", b=16)[:, :, t * 128:(t + 1) * 128],
                    xo.rearrange("p (b h) -> p b h", b=16),
                    transpose=True), waits=[apw] + list(extra_y),
                    inc16=f"dtp{slot}")
                xns_free[xns[i]] = (f"dtp{slot}", tp_cnt[slot] * 16)

        def do_taps(names_bases):
            dsems = [s for s in SEMS if s not in ("T", "V", "S")]
            w = ([("T", p.cnt["T"]), ("V", p.cnt["V"]), ("S", p.cnt["S"])] +
                 [(s, p.cnt[s]) for s in dsems])
            for nm, base, ncols in names_bases:
                d = ap(nc.dram_tensor(nm, [128, ncols], BF16, kind="ExternalOutput"))
                p.op("Y", lambda e, d=d, b=base, n=ncols: e.dma_start(d, B(b, n)),
                     waits=w, inc16="dout")
                w = []
            vfin = p.cnt["dout"]
            for eng in ("Y", "T", "V", "S", "G"):
                p.op(eng, lambda e, v=vfin: e.wait_ge(sems["dout"], v))

        def p1_spec(tc):
            s, rows = (d_xp, tc * 128) if tc < 8 else (d_xo, (tc - 8) * 128)
            base, tcl = (R0, tc) if tc < 8 else (XT_OWN, tc - 8)
            return (s, rows, tc % 2, base, tcl, tc % 4)
        for pr in range(8):
            ln_pair([p1_spec(2 * pr), p1_spec(2 * pr + 1)], ring="S")
        ev_tp0_p1, ev_tp1_p1 = tp_cnt[0] * 16, tp_cnt[1] * 16
        if stop_after == "p1":
            do_taps([("dbg_xtp", R0, 16384), ("dbg_xto", XT_OWN, 16384)])

        xt_tile = lambda hc, tb: B((R0 if tb < 2 else XT_OWN) + hc * 1024 + (tb % 2) * 512, 512)
        xt_ttile = lambda hc, tc: B((R0 if tc < 8 else XT_OWN) + hc * 1024 + (tc % 8) * 128, 128)

        def mm_group(bank, ncols, tiles, waits_first):
            w0 = list(waits_first)
            if bank_free[bank]:
                w0.append(bank_free[bank])
            n = len(tiles)
            ev = None
            for i, (lt, rt) in enumerate(tiles):
                ev = p.op("T", lambda e, l=lt, r=rt, st=(i == 0), sp=(i == n - 1),
                          b=bank, nc_=ncols: e.matmul(pb[b][:, 0:nc_], l, r, start=st, stop=sp),
                          waits=(w0 if i == 0 else ()), inc=("T" if i == n - 1 else None))
            return ev

        def evac(eng, bank, fn, waits):
            ev = p.op(eng, fn, waits=waits, inc=eng)
            bank_free[bank] = (eng, ev)
            return ev

        grp = 0
        ev_prev_T = None

        def tp_waits(chunks):
            n0 = max((ch // 2 + 1 for ch in chunks if ch % 2 == 0), default=0)
            n1 = max((ch // 2 + 1 for ch in chunks if ch % 2 == 1), default=0)
            return [("dtp0", n0 * 16), ("dtp1", n1 * 16)]

        def k_group(kv, tb, bank=None):
            nonlocal grp
            bank = (grp % 4) if bank is None else bank
            tiles = [(B(WK + hc * 512 + kv * 128, 128), xt_tile(hc, tb)) for hc in range(NHC)]
            evt = mm_group(bank, 512, tiles, tp_waits(range(tb * 4, tb * 4 + 4))
                           + [("dwk", ev_wvk["dwk"])])
            evac("S", bank, lambda e, b=bank, d=KS + kv * 2048 + tb * 512:
                 e.activation(B(d, 512), pb[b][:, :], AF.Identity, scale=SC),
                 waits=[("T", evt)])
            grp += 1
            return evt

        def v_group(tc, bank=None):
            nonlocal grp
            bank = (grp % 4) if bank is None else bank
            tiles = [(xt_ttile(hc, tc), B(WV + hc * 512, 512)) for hc in range(NHC)]
            evt = mm_group(bank, 512, tiles, tp_waits([tc]) + [("dwv", ev_wvk["dwv"])])
            evac("S", bank, lambda e, b=bank, d=VS + tc * 512:
                 e.copy(B(d, 512), pb[b][:, :]), waits=[("T", evt)])
            grp += 1
            return evt

        for tc in range(8):
            v_group(tc)
        for tb in range(2):
            for kv in range(NKV):
                ev_prev_T = k_group(kv, tb)
        for q in range(2):
            p.op("G", lambda e, q=q: e.dma_start(
                B(R0 + q * 8192, 8192).rearrange("p (b o) -> p b o", b=16),
                d_wq[:, q * 512:(q + 1) * 512].rearrange("(b p) o -> p b o", p=128)),
                waits=[("T", ev_prev_T)], inc16=f"dwq{q}")
        for tc in range(8, 12):
            v_group(tc)
        for kv in range(NKV):
            k_group(kv, 2)
        ev_q_half_last = None
        ev_q_evac = {}
        ev_q_quarter = {0: None, 1: None}
        for q in range(4):
            if q >= 2:
                p.op("G", lambda e, q=q: e.dma_start(
                    B(R0 + (q % 2) * 8192, 8192).rearrange("p (b o) -> p b o", b=16),
                    d_wq[:, q * 512:(q + 1) * 512].rearrange("(b p) o -> p b o", p=128)),
                    waits=[("T", ev_q_quarter[q % 2])], inc16=f"dwq{q}")
            for dloc2 in range(4):
                dloc = q * 4 + dloc2
                for tb in range(2):
                    bank = grp % 4
                    tiles = [(B(R0 + (q % 2) * 8192 + hc * 512 + dloc2 * 128, 128),
                              B(XT_OWN + hc * 1024 + tb * 512, 512)) for hc in range(NHC)]
                    evt = mm_group(bank, 512, tiles,
                                   [(f"dwq{q}", 16), ("dtp0", tp_cnt[0] * 16),
                                    ("dtp1", tp_cnt[1] * 16)])
                    ev_q_half_last = evt
                    ev_q_quarter[q % 2] = evt
                    ev_q_evac[dloc] = evac(
                        "S", bank, lambda e, b=bank, d=QS + dloc * 1024 + tb * 512:
                        e.copy(B(d, 512), pb[b][:, :]), waits=[("T", evt)])
                    grp += 1
        ev_dcg = precast(d_gwb, d_gw, INTER, "dcg")
        for q in range(2):
            p.op("G", lambda e, q=q: e.dma_start(
                B(R0 + q * 8192, 8192).rearrange("p (b o) -> p b o", b=16),
                d_wo[:, q * 512:(q + 1) * 512].rearrange("(b p) o -> p b o", p=128)),
                waits=[("T", ev_q_half_last)], inc16=f"dwo{q}")
        for i in range(4):
            p.op("G", lambda e, a=i * 512, b2=(i + 1) * 512: e.dma_start(
                d_wob[a:b2, :], d_wo[a:b2, 1024:2048]), inc16="dcwo")
        ev_dcwo = p.cnt["dcwo"]
        ev_dcu = precast(d_uwb, d_uw, INTER, "dcu")
        ev_dcd = precast(d_dwb, d_dw, H, "dcd")
        ev_S_P2 = p.cnt["S"]
        if stop_after == "p2":
            do_taps([("dbg_ks", KS, 8192), ("dbg_vs", VS, 8192), ("dbg_qs", QS, 16384)])

        ln_prev_ms = [None]

        ln_pending = {}

        def ln_stats_load(rc):
            ln_pending[rc] = load_x(d_xo, rc * 128, rc % 2, ring="Y")

        def ln_stats_chunk(src_ap, rows, slot, rc):
            c, ro, no = 16, 32 + rc * 2, 33 + rc * 2
            ev_x = ln_pending[rc]
            ev1 = p.op("V", lambda e, sl=slot, c=c: e.reduce_sum(
                ST[:, c:c + 1], XB[sl][:, :], AX.X), waits=[(f"dx{slot}", ev_x)], inc="V")
            ev_sq = p.op("S", lambda e, sl=slot, c=c, xo=B(XNT + slot * 2048, 2048):
                e.activation(xo, XB[sl][:, :], AF.Square, accum_out=ST[:, c + 1:c + 2]),
                waits=[(f"dx{slot}", ev_x), xns_free[slot], ln_prev_ms[0],
                       ("V", ev1)], inc="S")
            xb_free[slot] = ("S", ev_sq)
            ev1 = p.op("V", lambda e, c=c: e.tensor_scalar_mul(
                ST[:, c + 2:c + 3], ST[:, c:c + 1], 1.0 / H), waits=[("V", ev1)], inc="V")
            ev_ms = p.op("V", lambda e, c=c: e.tensor_scalar_mul(
                ST[:, c + 3:c + 4], ST[:, c + 1:c + 2], 1.0 / H),
                waits=[("S", ev_sq), ("V", ev1)], inc="V")
            ln_prev_ms[0] = ("V", ev_ms)
            ev1 = p.op("V", lambda e, c=c: e.tensor_mul(
                ST[:, c + 4:c + 5], ST[:, c + 2:c + 3], ST[:, c + 2:c + 3]),
                waits=[("V", ev_ms)], inc="V")
            ev1 = p.op("V", lambda e, c=c: e.scalar_tensor_tensor(
                ST[:, c + 4:c + 5], ST[:, c + 3:c + 4], EPS, ST[:, c + 4:c + 5],
                ALU.add, ALU.subtract), waits=[("V", ev1)], inc="V")
            ev_sd = p.op("S", lambda e, c=c: e.activation(
                ST[:, c + 5:c + 6], ST[:, c + 4:c + 5], AF.Ln),
                waits=[("V", ev1)], inc="S")
            ev_sd = p.op("S", lambda e, c=c, ro=ro: e.activation(
                ST[:, ro:ro + 1], ST[:, c + 5:c + 6], AF.Exp, scale=-0.5),
                waits=[("S", ev_sd)], inc="S")
            p.op("V", lambda e, c=c, ro=ro, no=no: e.scalar_tensor_tensor(
                ST[:, no:no + 1], ST[:, c + 2:c + 3], -1.0, ST[:, ro:ro + 1],
                ALU.mult, ALU.mult), waits=[("S", ev_sd)], inc="V")

        ev_O_evac_tc = {}
        o_state = {"last_T": None}
        o_rr = {"i": 0}

        def o_group(q, g, tl, banks, eng="S"):
            bank = banks[o_rr["i"] % len(banks)]
            o_rr["i"] += 1
            w0 = [(f"dwo{q}", 16), ("V", ev_attnevac[g * 16 + 15])]
            tiles = [(B(XT_OWN + g * 8192 + hc * 512 + tl * 128, 128),
                      B(R0 + (q % 2) * 8192 + hc * 512, 512)) for hc in range(NHC)]
            evt = mm_group(bank, 512, tiles, w0)
            o_state["last_T"] = evt
            o_state[("q", q)] = evt
            if eng == "V":
                fn = (lambda e, b=bank, d=AO + (g * 4 + tl) * 2048 + q * 512:
                      e.tensor_copy(B(d, 512), pb[b][:, :]))
            else:
                fn = (lambda e, b=bank, d=AO + (g * 4 + tl) * 2048 + q * 512:
                      e.copy(B(d, 512), pb[b][:, :]))
            ev_O_evac_tc[g * 4 + tl] = (eng, evac(eng, bank, fn, waits=[("T", evt)]))

        ev_attnevac = {}
        ev_ln = {}
        ev_lastQK = None
        ev_late_kv = [0]
        for g in range(2):
            for h in range(NH):
                hh = g * 16 + h
                kv = h // 4
                ab, lb = 2 + h % 2, 4 + h % 2
                ev_QK, ev_exp, ev_l, ev_msk = {}, {}, {}, {}

                def emit_qk(j, g=g, h=h, kv=kv):
                    kt = 4 * g + j
                    c0, c1 = max(0, j - 8) * 128, (min(3, j) + 1) * 128
                    w_qk = [("S", ev_q_evac[h])]
                    if g == 1 and j >= 8:
                        w_qk.append(("S", ev_late_kv[0]))
                    ev_QK[j] = p.op("T", lambda e, j=j, kt=kt, c0=c0, c1=c1, kv=kv, g=g, h=h:
                                    e.matmul(pb[j % 2][:, c0:c1],
                                             B(KS + kv * 2048 + kt * 128, 128),
                                             B(QS + h * 1024 + g * 512 + c0, c1 - c0),
                                             start=True, stop=True),
                                    waits=w_qk, inc="T")
                    if j <= 3 or j >= 8:
                        mc = 0 if j <= 3 else 128
                        blk = (j if j <= 3 else j - 8) * 128
                        ev_msk[j] = p.op("V", lambda e, j=j, blk=blk, mc=mc: e.tensor_tensor(
                            pb[j % 2][:, blk:blk + 128], pb[j % 2][:, blk:blk + 128],
                            MSK[:, mc:mc + 128], ALU.add),
                            waits=[("T", ev_QK[j]), ("dwc", ev_dwc)], inc="V")

                def emit_exp(j):
                    c0, c1 = max(0, j - 8) * 128, (min(3, j) + 1) * 128
                    w = [("V", ev_msk[j])] if j in ev_msk else [("T", ev_QK[j])]
                    if j >= 2:
                        w.append(("T", ev_l[j - 2]))
                    ev_exp[j] = p.op("S", lambda e, j=j, c0=c0, c1=c1: e.activation(
                        B(PS0 + (j % 2) * 512 + c0, c1 - c0), pb[j % 2][:, c0:c1], AF.Exp),
                        waits=w, inc="S")

                def emit_avl(j, g=g, kv=kv, ab=ab, lb=lb, hh=hh):
                    kt = 4 * g + j
                    c0, c1 = max(0, j - 8) * 128, (min(3, j) + 1) * 128
                    w_av = [("S", ev_exp[j])]
                    if j == 0 and (hh - 2) in ev_attnevac:
                        w_av.append(("V", ev_attnevac[hh - 2]))
                    p.op("T", lambda e, j=j, kt=kt, c0=c0, c1=c1, kv=kv, ab=ab: e.matmul(
                        pb[ab][:, c0:c1], B(VS + kt * 512 + kv * 128, 128),
                        B(PS0 + (j % 2) * 512 + c0, c1 - c0),
                        start=(j == 0), stop=(j == 11)), waits=w_av)
                    w_l = []
                    if j == 0:
                        if (hh - 2) in ev_ln:
                            w_l.append(("S", ev_ln[hh - 2]))
                        w_l.append(("dwc", ev_dwc))
                    ev_l[j] = p.op("T", lambda e, j=j, c0=c0, c1=c1, lb=lb: e.matmul(
                        pb[lb][:, c0:c1], B(ONES, 128),
                        B(PS0 + (j % 2) * 512 + c0, c1 - c0),
                        start=(j == 0), stop=(j == 11)), waits=w_l, inc="T")

                emit_qk(0)
                emit_qk(1)
                for j in range(12):
                    emit_exp(j)
                    emit_avl(j)
                    if j + 2 <= 11:
                        emit_qk(j + 2)
                ev_lastQK = ev_l[11]
                ev_phm = p.op("V", lambda e, lb=lb, g=g: e.tensor_tensor(
                    pb[lb][:, :], pb[lb][:, :], PH[:, g * 512:(g + 1) * 512], ALU.add),
                    waits=[("T", ev_l[11]), ("dwc", ev_dwc)], inc="V")
                ev_ln[hh] = p.op("S", lambda e, lb=lb: e.activation(
                    FAR[:, 0:512], pb[lb][:, :], AF.Ln), waits=[("V", ev_phm)], inc="S")
                w_rc = [("V", ev_attnevac[hh - 1])] if (hh - 1) in ev_attnevac else []
                ev_rcpb = p.op("S", lambda e: e.activation(
                    FAR[:, 512:1024], FAR[:, 0:512], AF.Exp, scale=-1.0),
                    waits=w_rc, inc="S")
                ev_attnevac[hh] = p.op("V", lambda e, g=g, h=h, ab=ab: e.tensor_tensor(
                    B(XT_OWN + g * 8192 + h * 512, 512), pb[ab][:, :],
                    FAR[:, 512:1024], ALU.mult), waits=[("S", ev_rcpb)], inc="V")
                if g == 0 and h % 2 == 1:
                    idx = h // 2
                    if idx < 4:
                        v_group(12 + idx, bank=6 + idx % 2)
                    else:
                        k_group(idx - 4, 3, bank=6 + idx % 2)
                    ev_late_kv[0] = p.cnt["S"]
                if g == 1 and h % 2 == 1:
                    idx = h // 2
                    o_group(idx // 4, 0, idx % 4, [6, 7])
                if g == 1 and h % 2 == 1 and h // 2 + 1 < 8:
                    ln_stats_load(h // 2 + 1)
                if g == 0 and h == 15:
                    ln_stats_load(0)
                if g == 1 and h % 2 == 0:
                    rc = h // 2
                    ln_stats_chunk(d_xo, rc * 128, rc % 2, rc)

        sP3 = p.cnt["S"]
        bank_free[0] = bank_free[1] = ("S", sP3)
        bank_free[4] = bank_free[5] = ("S", sP3)
        bank_free[2] = ("V", ev_attnevac[30])
        bank_free[3] = ("V", ev_attnevac[31])

        def wo_late_load(q, ring):
            p.op(ring, lambda e, q=q: e.dma_start(
                B(R0 + (q % 2) * 8192, 8192).rearrange("p (b o) -> p b o", b=16),
                d_wob[:, (q - 2) * 512:(q - 1) * 512]
                .rearrange("(b p) o -> p b o", p=128)),
                waits=[("T", o_state[("q", q - 2)]), ("dcwo", ev_dcwo)],
                inc16=f"dwo{q}")

        tp_after_rc = {}
        ev_r = {}

        def tail_apply(rc):
            slot = rc % 2
            ro, no = 32 + rc * 2, 33 + rc * 2
            ev_ap2 = p.op("V", lambda e, sl=slot, ro=ro, no=no,
                          xo=B(XNT + slot * 2048, 2048): e.tensor_scalar(
                xo, XB[sl][:, :], ST[:, ro:ro + 1], ST[:, no:no + 1],
                ALU.mult, ALU.add),
                waits=[(f"dx{slot}", ev_r[rc]), xns_free[slot], ("S", sP3)],
                inc="V")
            xb_free[slot] = ("V", ev_ap2)
            tp_cnt[slot] += 1
            p.op("Y", lambda e, sl=slot, t=rc: e.dma_start(
                B(XT_P4, 16384).rearrange("p (b t) -> p b t", b=16)[:, :, t * 128:(t + 1) * 128],
                B(XNT + sl * 2048, 2048).rearrange("p (b h) -> p b h", b=16),
                transpose=True), waits=[("V", ev_ap2), ("T", ev_lastQK)],
                inc16=f"dtp{slot}")
            xns_free[slot] = (f"dtp{slot}", tp_cnt[slot] * 16)
            if rc in (3, 7):
                tp_after_rc[rc] = (tp_cnt[0] * 16, tp_cnt[1] * 16)

        ev_r[0] = load_x(d_xo, 0 * 128, 0, ring="S")
        ev_r[1] = load_x(d_xo, 1 * 128, 1, ring="S")
        tail_apply(0)
        tail_apply(1)
        ev_r[2] = load_x(d_xo, 2 * 128, 0, ring="S")
        ev_r[3] = load_x(d_xo, 3 * 128, 1, ring="S")
        for tl in range(4):
            o_group(0, 1, tl, [0, 1, 6, 7], eng="V")
        wo_late_load(2, "Y")
        tail_apply(2)
        tail_apply(3)
        ev_r[4] = load_x(d_xo, 4 * 128, 0, ring="S")
        for tl in range(4):
            o_group(1, 1, tl, [0, 1, 6, 7], eng="V")
        wo_late_load(3, "S")
        tail_apply(4)
        ev_r[5] = load_x(d_xo, 5 * 128, 1, ring="S")
        for tl in range(4):
            o_group(2, 0, tl, [0, 1, 6, 7], eng="V")
        tail_apply(5)
        ev_r[6] = load_x(d_xo, 6 * 128, 0, ring="S")
        for tl in range(4):
            o_group(2, 1, tl, [0, 1, 6, 7], eng="V")
        ev_r[7] = load_x(d_xo, 7 * 128, 1, ring="S")
        for tl in range(4):
            o_group(3, 0, tl, [0, 1, 6, 7], eng="V")
        tail_apply(6)
        for tl in range(4):
            o_group(3, 1, tl, [0, 1, 6, 7], eng="V")
        tail_apply(7)
        ev_O_T_all = o_state["last_T"]
        gu_slot_free = {i: None for i in range(4)}
        gu_cnt = [0, 0, 0, 0]
        ev_gu_last = None
        sg_evs = {}
        sg_free = {0: None, 1: None}
        gu_mm_ev = {}

        def gu_transpose(ic):
            for wt in range(2):
                slot = wt + 2 * (ic % 2)
                base = GUW[slot]
                srcw = d_gwb if wt == 0 else d_uwb
                w_tp = [(("dcg", ev_dcg) if wt == 0 else ("dcu", ev_dcu)),
                        gu_slot_free[slot]]
                if ic < 2 and slot < 2:
                    w_tp.append(("T", ev_O_T_all))
                if ic == 2:
                    w_tp += [("dgu2", p.cnt["dgu2"]), ("dgu3", p.cnt["dgu3"])]
                gu_cnt[slot] += 16
                p.op("Y" if ic < 2 else "S",
                     lambda e, b=base, s=srcw, r=ic * 128: e.dma_start(
                    B(b, 2048).rearrange("p (m q) -> p m q", m=16),
                    s[r:r + 128, :].rearrange("p (m q) -> p m q", m=16),
                    transpose=True), waits=w_tp, inc16=f"dgu{slot}")

        gu_transpose(0)
        gu_transpose(1)
        for rc in range(8):
            slot = rc % 2
            ev_x2 = load_x(d_xo, rc * 128, slot, ring="S")
            ev_xa = p.op("V", lambda e, rc=rc, sl=slot: e.tensor_tensor(
                B(AO + rc * 2048, 2048), B(AO + rc * 2048, 2048), XB[sl][:, :],
                ALU.add), waits=[(f"dx{slot}", ev_x2), ev_O_evac_tc[rc]], inc="V")
            xb_free[slot] = ("V", ev_xa)
        ev_tp0_all, ev_tp1_all = tp_cnt[0] * 16, tp_cnt[1] * 16

        for ic in range(NIC):
            wtev = {}
            for wt in range(2):
                slot = wt + 2 * (ic % 2)
                base = GUW[slot]
                evs = []
                for tb in range(2):
                    bank = wt * 2 + tb
                    t0, t1 = tp_after_rc[3 if tb == 0 else 7]
                    w0 = [(f"dgu{slot}", gu_cnt[slot]), ("dtp0", t0), ("dtp1", t1)]
                    tiles = [(B(base + hc * 128, 128),
                              B(XT_P4 + hc * 1024 + tb * 512, 512)) for hc in range(NHC)]
                    evt = mm_group(bank, 512, tiles, w0)
                    evs.append(evt)
                    wtev[(wt, tb)] = evt
                    ev_gu_last = evt
                if wt == 1:
                    gu_mm_ev[ic] = evs[-1]
                    gu_slot_free[0 + 2 * (ic % 2)] = ("T", evs[-1])
                    gu_slot_free[1 + 2 * (ic % 2)] = ("T", evs[-1])
            for tb in range(2):
                ev_si = evac("S", 0 + tb,
                    lambda e, b=0 + tb, c=ic: e.activation(
                        B(PS0 + (b % 2) * 512, 512), pb[b][:, :], AF.Silu,
                        scale=GSC[:, c:c + 1]),
                    waits=[("T", wtev[(0, tb)]), ("dwc", ev_dwc), sg_free[tb]])
                sg_evs[tb] = ("S", ev_si)
            for tb in range(2):
                ev_h = evac("V", 2 + tb,
                    lambda e, b=2 + tb, c=ic, tb=tb: e.scalar_tensor_tensor(
                        B(hdn_base(c) + tb * 512, 512), pb[b][:, :], USC[:, c:c + 1],
                        B(PS0 + tb * 512, 512), ALU.mult, ALU.mult),
                    waits=[("T", wtev[(1, tb)]), sg_evs[tb], ("dwc", ev_dwc)])
                sg_free[tb] = ("V", ev_h)
            if ic + 2 < NIC:
                gu_transpose(ic + 2)
            if stop_after == "p4ic2" and ic == 1:
                do_taps([("dbg_guw", GUW[0], 2048), ("dbg_guw1", GUW[1], 2048),
                         ("dbg_guw2", GUW[2], 2048), ("dbg_guw3", GUW[3], 2048),
                         ("dbg_h01", 0, 2048), ("dbg_xto3", XT_OWN, 16384)])
                break

        if stop_after == "p4gu":
            do_taps([("dbg_hdnA", 0, 16384), ("dbg_hdnB", 49152, 16384),
                     ("dbg_hdnC", 65536, 12288)])
        dw_slot_free = {0: None, 1: None}
        dw_cnt = [0, 0]
        ev_down_last_T = None
        for hb in range(8):
            dsl = hb % 2
            w_tp = [("dcd", ev_dcd), ("T", ev_gu_last), dw_slot_free[dsl]]
            for hcs in range(2):
                hc = hb * 2 + hcs
                if dsl == 0:
                    dw_cnt[dsl] += 16
                    p.op("S", lambda e, hc=hc, par=hcs: e.dma_start(
                        B(DWA, 11264).rearrange("p (m q) -> p m q", m=NIC)[:, :, par * 128:par * 128 + 128],
                        d_dwb[hc * 128:(hc + 1) * 128, :].rearrange("p (m q) -> p m q", m=NIC),
                        transpose=True), waits=(w_tp if hcs == 0 else ()), inc16=f"ddw{dsl}")
                else:
                    for pi, (pbase, iclo, nic) in enumerate(DWB_PIECES):
                        dw_cnt[dsl] += 16
                        p.op("S", lambda e, hc=hc, par=hcs, b=pbase, lo=iclo, n=nic:
                             e.dma_start(
                                 B(b, n * 256).rearrange("p (m q) -> p m q", m=n)[:, :, par * 128:par * 128 + 128],
                                 d_dwb[hc * 128:(hc + 1) * 128, lo * 128:(lo + n) * 128]
                                 .rearrange("p (m q) -> p m q", m=n),
                                 transpose=True),
                             waits=(w_tp if (hcs == 0 and pi == 0) else ()), inc16=f"ddw{dsl}")
            ev_ddw = dw_cnt[dsl]
            stage = XB[dsl]
            for tl in range(8):
                bank = 4 + (hb * 8 + tl) % 4
                if dsl == 0:
                    tiles = [(B(hdn_base(ic) + tl * 128, 128),
                              B(DWA + ic * 256, 256)) for ic in range(NIC)]
                else:
                    tiles = [(B(hdn_base(ic) + tl * 128, 128),
                              B(dwb_tile(ic), 256)) for ic in range(NIC)]
                evt = mm_group(bank, 256, tiles, [(f"ddw{dsl}", ev_ddw)])
                ev_down_last_T = evt
                dw_slot_free[dsl] = ("T", evt)
                p.op("V", lambda e, b=bank, st=stage, tl=tl, hb=hb: e.tensor_tensor(
                    st[:, tl * 256:(tl + 1) * 256], pb[b][:, 0:256],
                    B(DS + hb * 256, 256), ALU.mult),
                    waits=[("T", evt), ("dwc", ev_dwc), xb_free[dsl]])
                ev_e2 = evac("V", bank,
                    lambda e, st=stage, tl=tl, hb=hb: e.tensor_tensor(
                        st[:, tl * 256:(tl + 1) * 256], st[:, tl * 256:(tl + 1) * 256],
                        B(AO + tl * 2048 + hb * 256, 256), ALU.add),
                    waits=[])
                p.op("Y", lambda e, st=stage, tl=tl, hb=hb: e.dma_start(
                    d_out[tl * 128:(tl + 1) * 128, hb * 256:(hb + 1) * 256],
                    st[:, tl * 256:(tl + 1) * 256]),
                    waits=[("V", ev_e2)], inc16="dout")
            xb_free[dsl] = ("dout", p.cnt["dout"])

        p.op("Y", lambda e: e.wait_ge(sems["dout"], p.cnt["dout"]))

        with nc.Block() as blk:
            def mk_body(lst):
                def body(eng):
                    for fn, w, inc, inc16 in lst:
                        for s, v in w:
                            eng.wait_ge(sems[s], v)
                        inst = fn(eng)
                        if inc:
                            inst.then_inc(sems[inc], 1)
                        if inc16:
                            inst.then_inc(sems[inc16], 16)
                return body
            blk.tensor(mk_body(p.ops["T"]))
            blk.vector(mk_body(p.ops["V"]))
            blk.scalar(mk_body(p.ops["S"]))
            blk.sync(mk_body(p.ops["Y"]))
            blk.gpsimd(mk_body(p.ops["G"]))
    return nc


def kernel(**inputs):
    f32 = np.float32
    x = np.ascontiguousarray(np.asarray(inputs["L0_attn_hidden_states"]).reshape(8192, 2048))
    wq = np.ascontiguousarray(inputs["L0_attn_Wq"])
    wk = np.ascontiguousarray(inputs["L0_attn_Wk"])
    wv = np.ascontiguousarray(inputs["L0_attn_Wv"])
    wo = np.ascontiguousarray(inputs["L0_attn_Wo"])
    gw = np.ascontiguousarray(inputs["L0_ffn_gate_w"])
    uw = np.ascontiguousarray(inputs["L0_ffn_up_w"])
    dw = np.ascontiguousarray(inputs["L0_ffn_down_w"])
    gs = np.ascontiguousarray(np.asarray(inputs["L0_ffn_gate_s"]).reshape(NIC, 128).T)
    us = np.ascontiguousarray(np.asarray(inputs["L0_ffn_up_s"]).reshape(NIC, 128).T)
    ds = np.ascontiguousarray(np.broadcast_to(
        np.asarray(inputs["L0_ffn_down_s"])[None, :], (128, 2048)))
    kr = np.arange(128)[:, None]
    qc = np.arange(128)[None, :]
    msk = np.ascontiguousarray(np.concatenate([
        np.where(qc <= kr, 0.0, -1e30),
        np.where(kr <= qc, 0.0, -1e30),
    ], axis=1).astype(f32))
    ones = np.ones((128, 128), f32)
    zeros_ph = np.zeros((128, 1024), f32)
    ph0 = np.ascontiguousarray(np.broadcast_to(
        (np.arange(1024) - 1024.0).astype(f32)[None, :], (128, 1024)))

    nc = build_program()
    in_maps = []
    for c in range(8):
        x_own = np.ascontiguousarray(x[c * TOK:(c + 1) * TOK])
        x_prev = (np.ascontiguousarray(x[(c - 1) * TOK:c * TOK]) if c > 0
                  else np.zeros((TOK, H), f32))
        in_maps.append(dict(
            x_own=x_own, x_prev=x_prev,
            wq=wq, wk=wk, wv=wv, wo=wo, gate_w=gw, up_w=uw, down_w=dw,
            gate_s=gs, up_s=us, down_s_rep=ds, masks=msk,
            phantom=(ph0 if c == 0 else zeros_ph), ones_in=ones))
    res = run_bass_kernel_spmd(nc, in_maps, core_ids=list(range(8)))
    out = np.concatenate([r["out"] for r in res.results], axis=0)
    return out.reshape(1, 8192, 2048).astype(np.float32, copy=False)


# version 43
# speedup vs baseline: 1.0292x; 1.0292x over previous
import contextlib
import numpy as np

import concourse.bass as bass
import concourse.mybir as mybir
from concourse.bass_utils import run_bass_kernel_spmd

F32 = mybir.dt.float32
BF16 = mybir.dt.bfloat16
AF = mybir.ActivationFunctionType
ALU = mybir.AluOpType
AX = mybir.AxisListType

TOK = 1024
H = 2048
NHC = 16
NH = 16
NKV = 4
INTER = 5632
NIC = 44
SC = 0.08838834764831845
EPS = 1e-5

NBIG = 90624
R0 = 0
XT_OWN = 16384
WK = 32768
WV = 40960
AO = 32768
KS = 49152
VS = 57344
QS = 65536
PS0 = 81920
ONES = 82944
XNT = 84480
DS = 88576
GUW = (28672, 30720, 84480, 86528)
XT_P4 = 65536
DWA = 65536
DWB_PIECES = ((76800, 0, 20), (84480, 20, 16), (28672, 36, 8))


def hdn_base(ic):
    if ic < 16:
        return ic * 1024
    if ic < 32:
        return 49152 + (ic - 16) * 1024
    return 16384 + (ic - 32) * 1024


def dwb_tile(ic):
    for base, lo, n in DWB_PIECES:
        if lo <= ic < lo + n:
            return base + (ic - lo) * 256
    raise AssertionError

SEMS = ["T", "V", "S", "dx0", "dx1", "dtp0", "dtp1", "dwc", "dwv", "dwk",
        "dwq0", "dwq1", "dwq2", "dwq3", "dwo0", "dwo1", "dwo2", "dwo3",
        "dcwo", "dcg", "dcu", "dcd",
        "dgu0", "dgu1", "dgu2", "dgu3", "ddw0", "ddw1", "dout"]


class P:
    def __init__(self):
        self.cnt = {s: 0 for s in SEMS}
        self.ops = {e: [] for e in "TVSYG"}
        self.lastw = {e: {} for e in "TVSYG"}

    def op(self, eng, fn, waits=(), inc=None, inc16=None):
        w = []
        for item in waits:
            if item is None:
                continue
            s, v = item
            if v is None or v <= 0:
                continue
            if self.lastw[eng].get(s, -1) >= v:
                continue
            w.append((s, v))
            self.lastw[eng][s] = v
        if inc:
            self.cnt[inc] += 1
        if inc16:
            self.cnt[inc16] += 16
        self.ops[eng].append((fn, w, inc, inc16))
        return self.cnt[inc] if inc else (self.cnt[inc16] if inc16 else None)


def build_program(stop_after=None):
    nc = bass.Bass("TRN2", target_bir_lowering=False)
    ap = lambda t: t.ap()
    d_xo = ap(nc.dram_tensor("x_own", [TOK, H], F32, kind="ExternalInput"))
    d_xp = ap(nc.dram_tensor("x_prev", [TOK, H], F32, kind="ExternalInput"))
    d_wq = ap(nc.dram_tensor("wq", [H, H], F32, kind="ExternalInput"))
    d_wk = ap(nc.dram_tensor("wk", [H, 512], F32, kind="ExternalInput"))
    d_wv = ap(nc.dram_tensor("wv", [H, 512], F32, kind="ExternalInput"))
    d_wo = ap(nc.dram_tensor("wo", [H, H], F32, kind="ExternalInput"))
    d_gw = ap(nc.dram_tensor("gate_w", [INTER, H], F32, kind="ExternalInput"))
    d_uw = ap(nc.dram_tensor("up_w", [INTER, H], F32, kind="ExternalInput"))
    d_dw = ap(nc.dram_tensor("down_w", [H, INTER], F32, kind="ExternalInput"))
    d_gs = ap(nc.dram_tensor("gate_s", [128, NIC], F32, kind="ExternalInput"))
    d_us = ap(nc.dram_tensor("up_s", [128, NIC], F32, kind="ExternalInput"))
    d_ds = ap(nc.dram_tensor("down_s_rep", [128, H], F32, kind="ExternalInput"))
    d_msk = ap(nc.dram_tensor("masks", [128, 256], F32, kind="ExternalInput"))
    d_ph = ap(nc.dram_tensor("phantom", [128, 1024], F32, kind="ExternalInput"))
    d_ones = ap(nc.dram_tensor("ones_in", [128, 128], F32, kind="ExternalInput"))
    d_out = ap(nc.dram_tensor("out", [TOK, H], F32, kind="ExternalOutput"))
    d_gwb = ap(nc.dram_tensor("gw_bf", [INTER, H], BF16, kind="Internal"))
    d_uwb = ap(nc.dram_tensor("uw_bf", [INTER, H], BF16, kind="Internal"))
    d_dwb = ap(nc.dram_tensor("dw_bf", [H, INTER], BF16, kind="Internal"))
    d_wob = ap(nc.dram_tensor("wo_bf", [H, 1024], BF16, kind="Internal"))

    p = P()
    with contextlib.ExitStack() as ctx:
        sems = {s: ctx.enter_context(nc.semaphore(s)) for s in SEMS}
        BIG = ctx.enter_context(nc.sbuf_tensor("BIG", [128, NBIG], BF16))
        XB = [ctx.enter_context(nc.sbuf_tensor(f"xb{i}", [128, H], F32)) for i in range(2)]
        FAR = ctx.enter_context(nc.sbuf_tensor("far", [128, 1024], F32))
        ST = ctx.enter_context(nc.sbuf_tensor("stats", [128, 64], F32))
        MSK = ctx.enter_context(nc.sbuf_tensor("msk", [128, 256], F32))
        PH = ctx.enter_context(nc.sbuf_tensor("ph", [128, 1024], F32))
        GSC = ctx.enter_context(nc.sbuf_tensor("gsc", [128, NIC], F32))
        USC = ctx.enter_context(nc.sbuf_tensor("usc", [128, NIC], F32))
        pb = [ctx.enter_context(nc.psum_tensor(f"pb{i}", [128, 512], F32)) for i in range(8)]

        B = lambda a, n: BIG[:, a:a + n]

        bank_free = {i: None for i in range(8)}
        xb_free = [None, None]
        xb_cnt = [0, 0]
        tp_cnt = [0, 0]
        XNS = (XNT, XNT + 2048, 65536, 67584)
        xns_free = {i: None for i in range(4)}

        def load_x(src_ap, rows, slot, ring="Y"):
            xb_cnt[slot] += 1
            p.op(ring, lambda e, s=src_ap, r=rows, sl=slot:
                 e.dma_start(XB[sl][:, :], s[r:r + 128, :]),
                 waits=[xb_free[slot]], inc16=f"dx{slot}")
            return xb_cnt[slot] * 16

        p.op("G", lambda e: e.dma_start(B(ONES, 128), d_ones), inc16="dwc")
        p.op("G", lambda e: e.dma_start(B(DS, H), d_ds), inc16="dwc")
        p.op("Y", lambda e: e.dma_start(MSK[:, :], d_msk), inc16="dwc")
        p.op("Y", lambda e: e.dma_start(PH[:, :], d_ph), inc16="dwc")
        p.op("Y", lambda e: e.dma_start(GSC[:, :], d_gs), inc16="dwc")
        p.op("Y", lambda e: e.dma_start(USC[:, :], d_us), inc16="dwc")
        ev_dwc = p.cnt["dwc"]
        ev_wvk = {}
        for dst, src2, sem in ((WV, d_wv, "dwv"), (WK, d_wk, "dwk")):
            p.op("G", lambda e, d=dst, s=src2: e.dma_start(
                B(d, 8192).rearrange("p (b o) -> p b o", b=16),
                s.rearrange("(b p) o -> p b o", p=128)), inc16=sem)
            ev_wvk[sem] = p.cnt[sem]
        def precast(dst, srcw, rows, sem):
            qr = rows // 4
            for i in range(4):
                p.op("G", lambda e, d=dst, s=srcw, a=i * qr, b2=(i + 1) * qr:
                     e.dma_start(d[a:b2, :], s[a:b2, :]), inc16=sem)
            return p.cnt[sem]

        def ln_pair(specs, extra_s=(), extra_y=(), ring="Y"):
            n = len(specs)
            evx, evsq, xnt, ev = [None] * n, [None] * n, [None] * n, [None] * n
            xns = [sp[5] if len(sp) > 5 else sp[2] for sp in specs]
            for i, sp in enumerate(specs):
                evx[i] = load_x(sp[0], sp[1], sp[2], ring=ring)
            for i, sp in enumerate(specs):
                slot, c = sp[2], sp[2] * 8
                ev[i] = p.op("V", lambda e, sl=slot, c=c: e.reduce_sum(
                    ST[:, c:c + 1], XB[sl][:, :], AX.X),
                    waits=[(f"dx{slot}", evx[i])], inc="V")
            for i, sp in enumerate(specs):
                slot, c = sp[2], sp[2] * 8
                xnt[i] = B(XNS[xns[i]], 2048)
                evsq[i] = p.op("S", lambda e, sl=slot, c=c, xo=xnt[i]: e.activation(
                    xo, XB[sl][:, :], AF.Square, accum_out=ST[:, c + 1:c + 2]),
                    waits=[(f"dx{slot}", evx[i]), xns_free[xns[i]]]
                    + list(extra_s), inc="S")
            def step(i, fn, waits=()):
                slot = specs[i][2]
                ev[i] = p.op("V", fn(slot * 8), waits=[("V", ev[i])] + list(waits), inc="V")
            for i in range(n):
                step(i, lambda c: lambda e: e.tensor_scalar_mul(
                    ST[:, c + 2:c + 3], ST[:, c:c + 1], 1.0 / H))
            for i in range(n):
                step(i, lambda c: lambda e: e.tensor_scalar_mul(
                    ST[:, c + 3:c + 4], ST[:, c + 1:c + 2], 1.0 / H),
                    waits=[("S", evsq[i])])
            for i in range(n):
                step(i, lambda c: lambda e: e.tensor_mul(
                    ST[:, c + 4:c + 5], ST[:, c + 2:c + 3], ST[:, c + 2:c + 3]))
            for i in range(n):
                step(i, lambda c: lambda e: e.scalar_tensor_tensor(
                    ST[:, c + 4:c + 5], ST[:, c + 3:c + 4], EPS, ST[:, c + 4:c + 5],
                    ALU.add, ALU.subtract))
            evsd = [None] * n
            for i in range(n):
                c = specs[i][2] * 8
                evsd[i] = p.op("S", lambda e, c=c: e.activation(
                    ST[:, c + 5:c + 6], ST[:, c + 4:c + 5], AF.Sqrt),
                    waits=[("V", ev[i])], inc="S")
            for i in range(n):
                step(i, lambda c: lambda e: e.reciprocal(
                    ST[:, c + 6:c + 7], ST[:, c + 5:c + 6]), waits=[("S", evsd[i])])
            for i in range(n):
                step(i, lambda c: lambda e: e.scalar_tensor_tensor(
                    ST[:, c + 7:c + 8], ST[:, c + 2:c + 3], -1.0, ST[:, c + 6:c + 7],
                    ALU.mult, ALU.mult))
            for i, sp in enumerate(specs):
                src_ap, rows, slot, dest_base, tcl = sp[:5]
                c = slot * 8
                if i % 2 == 0:
                    ev_ap2 = p.op("V", lambda e, sl=slot, c=c, xo=xnt[i]: e.tensor_scalar(
                        xo, XB[sl][:, :], ST[:, c + 6:c + 7], ST[:, c + 7:c + 8],
                        ALU.mult, ALU.add), waits=[("V", ev[i])], inc="V")
                    xb_free[slot] = ("V", ev_ap2)
                    apw = ("V", ev_ap2)
                else:
                    ev_ap2 = p.op("S", lambda e, sl=slot, c=c, xo=xnt[i]: e.activation(
                        xo, XB[sl][:, :], AF.Identity, bias=ST[:, c + 7:c + 8],
                        scale=ST[:, c + 6:c + 7]), waits=[("V", ev[i])], inc="S")
                    xb_free[slot] = ("S", ev_ap2)
                    apw = ("S", ev_ap2)
                tp_cnt[slot] += 1
                p.op("Y", lambda e, xo=xnt[i], db=dest_base, t=tcl: e.dma_start(
                    B(db, 16384).rearrange("p (b t) -> p b t", b=16)[:, :, t * 128:(t + 1) * 128],
                    xo.rearrange("p (b h) -> p b h", b=16),
                    transpose=True), waits=[apw] + list(extra_y),
                    inc16=f"dtp{slot}")
                xns_free[xns[i]] = (f"dtp{slot}", tp_cnt[slot] * 16)

        def do_taps(names_bases):
            dsems = [s for s in SEMS if s not in ("T", "V", "S")]
            w = ([("T", p.cnt["T"]), ("V", p.cnt["V"]), ("S", p.cnt["S"])] +
                 [(s, p.cnt[s]) for s in dsems])
            for nm, base, ncols in names_bases:
                d = ap(nc.dram_tensor(nm, [128, ncols], BF16, kind="ExternalOutput"))
                p.op("Y", lambda e, d=d, b=base, n=ncols: e.dma_start(d, B(b, n)),
                     waits=w, inc16="dout")
                w = []
            vfin = p.cnt["dout"]
            for eng in ("Y", "T", "V", "S", "G"):
                p.op(eng, lambda e, v=vfin: e.wait_ge(sems["dout"], v))

        def p1_spec(tc):
            s, rows = (d_xp, tc * 128) if tc < 8 else (d_xo, (tc - 8) * 128)
            base, tcl = (R0, tc) if tc < 8 else (XT_OWN, tc - 8)
            return (s, rows, tc % 2, base, tcl, tc % 4)
        for pr in range(8):
            ln_pair([p1_spec(2 * pr), p1_spec(2 * pr + 1)], ring="S")
        ev_tp0_p1, ev_tp1_p1 = tp_cnt[0] * 16, tp_cnt[1] * 16
        if stop_after == "p1":
            do_taps([("dbg_xtp", R0, 16384), ("dbg_xto", XT_OWN, 16384)])

        xt_tile = lambda hc, tb: B((R0 if tb < 2 else XT_OWN) + hc * 1024 + (tb % 2) * 512, 512)
        xt_ttile = lambda hc, tc: B((R0 if tc < 8 else XT_OWN) + hc * 1024 + (tc % 8) * 128, 128)

        def mm_group(bank, ncols, tiles, waits_first):
            w0 = list(waits_first)
            if bank_free[bank]:
                w0.append(bank_free[bank])
            n = len(tiles)
            ev = None
            for i, (lt, rt) in enumerate(tiles):
                ev = p.op("T", lambda e, l=lt, r=rt, st=(i == 0), sp=(i == n - 1),
                          b=bank, nc_=ncols: e.matmul(pb[b][:, 0:nc_], l, r, start=st, stop=sp),
                          waits=(w0 if i == 0 else ()), inc=("T" if i == n - 1 else None))
            return ev

        def evac(eng, bank, fn, waits):
            ev = p.op(eng, fn, waits=waits, inc=eng)
            bank_free[bank] = (eng, ev)
            return ev

        grp = 0
        ev_prev_T = None

        def tp_waits(chunks):
            n0 = max((ch // 2 + 1 for ch in chunks if ch % 2 == 0), default=0)
            n1 = max((ch // 2 + 1 for ch in chunks if ch % 2 == 1), default=0)
            return [("dtp0", n0 * 16), ("dtp1", n1 * 16)]

        def k_group(kv, tb, bank=None):
            nonlocal grp
            bank = (grp % 4) if bank is None else bank
            tiles = [(B(WK + hc * 512 + kv * 128, 128), xt_tile(hc, tb)) for hc in range(NHC)]
            evt = mm_group(bank, 512, tiles, tp_waits(range(tb * 4, tb * 4 + 4))
                           + [("dwk", ev_wvk["dwk"])])
            evac("S", bank, lambda e, b=bank, d=KS + kv * 2048 + tb * 512:
                 e.activation(B(d, 512), pb[b][:, :], AF.Identity, scale=SC),
                 waits=[("T", evt)])
            grp += 1
            return evt

        def v_group(tc, bank=None):
            nonlocal grp
            bank = (grp % 4) if bank is None else bank
            tiles = [(xt_ttile(hc, tc), B(WV + hc * 512, 512)) for hc in range(NHC)]
            evt = mm_group(bank, 512, tiles, tp_waits([tc]) + [("dwv", ev_wvk["dwv"])])
            evac("S", bank, lambda e, b=bank, d=VS + tc * 512:
                 e.copy(B(d, 512), pb[b][:, :]), waits=[("T", evt)])
            grp += 1
            return evt

        for tc in range(8):
            v_group(tc)
        for tb in range(2):
            for kv in range(NKV):
                ev_prev_T = k_group(kv, tb)
        for q in range(2):
            p.op("G", lambda e, q=q: e.dma_start(
                B(R0 + q * 8192, 8192).rearrange("p (b o) -> p b o", b=16),
                d_wq[:, q * 512:(q + 1) * 512].rearrange("(b p) o -> p b o", p=128)),
                waits=[("T", ev_prev_T)], inc16=f"dwq{q}")
        for tc in range(8, 12):
            v_group(tc)
        for kv in range(NKV):
            k_group(kv, 2)
        ev_q_half_last = None
        ev_q_evac = {}
        ev_q_quarter = {0: None, 1: None}
        for q in range(4):
            if q >= 2:
                p.op("G", lambda e, q=q: e.dma_start(
                    B(R0 + (q % 2) * 8192, 8192).rearrange("p (b o) -> p b o", b=16),
                    d_wq[:, q * 512:(q + 1) * 512].rearrange("(b p) o -> p b o", p=128)),
                    waits=[("T", ev_q_quarter[q % 2])], inc16=f"dwq{q}")
            for dloc2 in range(4):
                dloc = q * 4 + dloc2
                for tb in range(2):
                    bank = grp % 4
                    tiles = [(B(R0 + (q % 2) * 8192 + hc * 512 + dloc2 * 128, 128),
                              B(XT_OWN + hc * 1024 + tb * 512, 512)) for hc in range(NHC)]
                    evt = mm_group(bank, 512, tiles,
                                   [(f"dwq{q}", 16), ("dtp0", tp_cnt[0] * 16),
                                    ("dtp1", tp_cnt[1] * 16)])
                    ev_q_half_last = evt
                    ev_q_quarter[q % 2] = evt
                    ev_q_evac[dloc] = evac(
                        "S", bank, lambda e, b=bank, d=QS + dloc * 1024 + tb * 512:
                        e.copy(B(d, 512), pb[b][:, :]), waits=[("T", evt)])
                    grp += 1
        ev_dcg = precast(d_gwb, d_gw, INTER, "dcg")
        for q in range(2):
            p.op("G", lambda e, q=q: e.dma_start(
                B(R0 + q * 8192, 8192).rearrange("p (b o) -> p b o", b=16),
                d_wo[:, q * 512:(q + 1) * 512].rearrange("(b p) o -> p b o", p=128)),
                waits=[("T", ev_q_half_last)], inc16=f"dwo{q}")
        for i in range(4):
            p.op("G", lambda e, a=i * 512, b2=(i + 1) * 512: e.dma_start(
                d_wob[a:b2, :], d_wo[a:b2, 1024:2048]), inc16="dcwo")
        ev_dcwo = p.cnt["dcwo"]
        ev_dcu = precast(d_uwb, d_uw, INTER, "dcu")
        ev_dcd = precast(d_dwb, d_dw, H, "dcd")
        ev_S_P2 = p.cnt["S"]
        if stop_after == "p2":
            do_taps([("dbg_ks", KS, 8192), ("dbg_vs", VS, 8192), ("dbg_qs", QS, 16384)])

        ln_prev_ms = [None]

        ln_pending = {}

        def ln_stats_load(rc):
            ln_pending[rc] = load_x(d_xo, rc * 128, rc % 2, ring="Y")

        ln_s_pending = []
        ln_rest_pending = []
        ln_prev_spart = {0: None, 1: None}

        def ln_stats_chunk(src_ap, rows, slot, rc):
            c, ro, no = 16 + (rc % 2) * 8, 32 + rc * 2, 33 + rc * 2
            ev_x = ln_pending[rc]
            ev_red = p.op("V", lambda e, sl=slot, c=c: e.reduce_sum(
                ST[:, c:c + 1], XB[sl][:, :], AX.X), waits=[(f"dx{slot}", ev_x)], inc="V")
            ev_sq = p.op("S", lambda e, sl=slot, c=c, xo=B(XNT + slot * 2048, 2048):
                e.activation(xo, XB[sl][:, :], AF.Square, accum_out=ST[:, c + 1:c + 2]),
                waits=[(f"dx{slot}", ev_x), xns_free[slot], ln_prev_ms[0],
                       ("V", ev_red)], inc="S")
            xb_free[slot] = ("S", ev_sq)

            def rest(c=c, ro=ro, no=no, ev_red=ev_red, ev_sq=ev_sq, rc=rc):
                ev1 = p.op("V", lambda e: e.tensor_scalar_mul(
                    ST[:, c + 2:c + 3], ST[:, c:c + 1], 1.0 / H),
                    waits=[("V", ev_red)], inc="V")
                ev_ms = p.op("V", lambda e: e.tensor_scalar_mul(
                    ST[:, c + 3:c + 4], ST[:, c + 1:c + 2], 1.0 / H),
                    waits=[("S", ev_sq), ("V", ev1)], inc="V")
                ln_prev_ms[0] = ("V", ev_ms)
                ev1 = p.op("V", lambda e: e.tensor_mul(
                    ST[:, c + 4:c + 5], ST[:, c + 2:c + 3], ST[:, c + 2:c + 3]),
                    waits=[("V", ev_ms), ln_prev_spart[rc % 2]], inc="V")
                ev1 = p.op("V", lambda e: e.scalar_tensor_tensor(
                    ST[:, c + 4:c + 5], ST[:, c + 3:c + 4], EPS, ST[:, c + 4:c + 5],
                    ALU.add, ALU.subtract), waits=[("V", ev1)], inc="V")

                def s_part():
                    ev_sd = p.op("S", lambda e: e.activation(
                        ST[:, c + 5:c + 6], ST[:, c + 4:c + 5], AF.Ln),
                        waits=[("V", ev1)], inc="S")
                    ev_sd = p.op("S", lambda e: e.activation(
                        ST[:, ro:ro + 1], ST[:, c + 5:c + 6], AF.Exp, scale=-0.5),
                        waits=[("S", ev_sd)], inc="S")
                    ln_prev_spart[rc % 2] = ("S", ev_sd)
                    p.op("V", lambda e: e.scalar_tensor_tensor(
                        ST[:, no:no + 1], ST[:, c + 2:c + 3], -1.0, ST[:, ro:ro + 1],
                        ALU.mult, ALU.mult), waits=[("S", ev_sd)], inc="V")
                ln_s_pending.append(s_part)
            ln_rest_pending.append(rest)

        ev_O_evac_tc = {}
        o_state = {"last_T": None}
        o_rr = {"i": 0}

        def o_group(q, g, tl, banks, eng="S"):
            bank = banks[o_rr["i"] % len(banks)]
            o_rr["i"] += 1
            w0 = [(f"dwo{q}", 16), ("V", ev_attnevac[g * 16 + 15])]
            tiles = [(B(XT_OWN + g * 8192 + hc * 512 + tl * 128, 128),
                      B(R0 + (q % 2) * 8192 + hc * 512, 512)) for hc in range(NHC)]
            evt = mm_group(bank, 512, tiles, w0)
            o_state["last_T"] = evt
            o_state[("q", q)] = evt
            if eng == "V":
                fn = (lambda e, b=bank, d=AO + (g * 4 + tl) * 2048 + q * 512:
                      e.tensor_copy(B(d, 512), pb[b][:, :]))
            else:
                fn = (lambda e, b=bank, d=AO + (g * 4 + tl) * 2048 + q * 512:
                      e.copy(B(d, 512), pb[b][:, :]))
            ev_O_evac_tc[g * 4 + tl] = (eng, evac(eng, bank, fn, waits=[("T", evt)]))

        ev_attnevac = {}
        ev_ln = {}
        ev_lastQK = None
        ev_late_kv = [0]
        for g in range(2):
            for h in range(NH):
                hh = g * 16 + h
                kv = h // 4
                ab, lb = 2 + h % 2, 4 + h % 2
                ev_QK, ev_exp, ev_l, ev_msk = {}, {}, {}, {}

                def emit_qk(j, g=g, h=h, kv=kv):
                    kt = 4 * g + j
                    c0, c1 = max(0, j - 8) * 128, (min(3, j) + 1) * 128
                    w_qk = [("S", ev_q_evac[h])]
                    if g == 1 and j >= 8:
                        w_qk.append(("S", ev_late_kv[0]))
                    ev_QK[j] = p.op("T", lambda e, j=j, kt=kt, c0=c0, c1=c1, kv=kv, g=g, h=h:
                                    e.matmul(pb[j % 2][:, c0:c1],
                                             B(KS + kv * 2048 + kt * 128, 128),
                                             B(QS + h * 1024 + g * 512 + c0, c1 - c0),
                                             start=True, stop=True),
                                    waits=w_qk, inc="T")
                    if j <= 3 or j >= 8:
                        mc = 0 if j <= 3 else 128
                        blk = (j if j <= 3 else j - 8) * 128
                        ev_msk[j] = p.op("V", lambda e, j=j, blk=blk, mc=mc: e.tensor_tensor(
                            pb[j % 2][:, blk:blk + 128], pb[j % 2][:, blk:blk + 128],
                            MSK[:, mc:mc + 128], ALU.add),
                            waits=[("T", ev_QK[j]), ("dwc", ev_dwc)], inc="V")

                def emit_exp(j):
                    c0, c1 = max(0, j - 8) * 128, (min(3, j) + 1) * 128
                    w = [("V", ev_msk[j])] if j in ev_msk else [("T", ev_QK[j])]
                    if j >= 2:
                        w.append(("T", ev_l[j - 2]))
                    ev_exp[j] = p.op("S", lambda e, j=j, c0=c0, c1=c1: e.activation(
                        B(PS0 + (j % 2) * 512 + c0, c1 - c0), pb[j % 2][:, c0:c1], AF.Exp),
                        waits=w, inc="S")

                def emit_avl(j, g=g, kv=kv, ab=ab, lb=lb, hh=hh):
                    kt = 4 * g + j
                    c0, c1 = max(0, j - 8) * 128, (min(3, j) + 1) * 128
                    w_av = [("S", ev_exp[j])]
                    if j == 0 and (hh - 2) in ev_attnevac:
                        w_av.append(("V", ev_attnevac[hh - 2]))
                    p.op("T", lambda e, j=j, kt=kt, c0=c0, c1=c1, kv=kv, ab=ab: e.matmul(
                        pb[ab][:, c0:c1], B(VS + kt * 512 + kv * 128, 128),
                        B(PS0 + (j % 2) * 512 + c0, c1 - c0),
                        start=(j == 0), stop=(j == 11)), waits=w_av)
                    w_l = []
                    if j == 0:
                        if (hh - 2) in ev_ln:
                            w_l.append(("S", ev_ln[hh - 2]))
                        w_l.append(("dwc", ev_dwc))
                    ev_l[j] = p.op("T", lambda e, j=j, c0=c0, c1=c1, lb=lb: e.matmul(
                        pb[lb][:, c0:c1], B(ONES, 128),
                        B(PS0 + (j % 2) * 512 + c0, c1 - c0),
                        start=(j == 0), stop=(j == 11)), waits=w_l, inc="T")

                emit_qk(0)
                emit_qk(1)
                for j in range(12):
                    emit_exp(j)
                    emit_avl(j)
                    if j + 2 <= 11:
                        emit_qk(j + 2)
                ev_lastQK = ev_l[11]
                ev_phm = p.op("V", lambda e, lb=lb, g=g: e.tensor_tensor(
                    pb[lb][:, :], pb[lb][:, :], PH[:, g * 512:(g + 1) * 512], ALU.add),
                    waits=[("T", ev_l[11]), ("dwc", ev_dwc)], inc="V")
                ev_ln[hh] = p.op("S", lambda e, lb=lb: e.activation(
                    FAR[:, 0:512], pb[lb][:, :], AF.Ln), waits=[("V", ev_phm)], inc="S")
                w_rc = [("V", ev_attnevac[hh - 1])] if (hh - 1) in ev_attnevac else []
                ev_rcpb = p.op("S", lambda e: e.activation(
                    FAR[:, 512:1024], FAR[:, 0:512], AF.Exp, scale=-1.0),
                    waits=w_rc, inc="S")
                ev_attnevac[hh] = p.op("V", lambda e, g=g, h=h, ab=ab: e.tensor_tensor(
                    B(XT_OWN + g * 8192 + h * 512, 512), pb[ab][:, :],
                    FAR[:, 512:1024], ALU.mult), waits=[("S", ev_rcpb)], inc="V")
                if g == 0 and h % 2 == 1:
                    idx = h // 2
                    if idx < 4:
                        v_group(12 + idx, bank=6 + idx % 2)
                    else:
                        k_group(idx - 4, 3, bank=6 + idx % 2)
                    ev_late_kv[0] = p.cnt["S"]
                if g == 1 and h % 2 == 1:
                    idx = h // 2
                    o_group(idx // 4, 0, idx % 4, [6, 7])
                if g == 1 and h % 2 == 1:
                    sp, ln_s_pending[:] = list(ln_s_pending), []
                    for fn in sp:
                        fn()
                    rp, ln_rest_pending[:] = list(ln_rest_pending), []
                    for fn in rp:
                        fn()
                if g == 1 and h % 2 == 1 and h // 2 + 1 < 8:
                    ln_stats_load(h // 2 + 1)
                if g == 0 and h == 15:
                    ln_stats_load(0)
                if g == 1 and h % 2 == 0:
                    rc = h // 2
                    ln_stats_chunk(d_xo, rc * 128, rc % 2, rc)

        for fn in list(ln_rest_pending):
            fn()
        ln_rest_pending.clear()
        for fn in list(ln_s_pending):
            fn()
        ln_s_pending.clear()
        sP3 = p.cnt["S"]
        bank_free[0] = bank_free[1] = ("S", sP3)
        bank_free[4] = bank_free[5] = ("S", sP3)
        bank_free[2] = ("V", ev_attnevac[30])
        bank_free[3] = ("V", ev_attnevac[31])

        def wo_late_load(q, ring):
            p.op(ring, lambda e, q=q: e.dma_start(
                B(R0 + (q % 2) * 8192, 8192).rearrange("p (b o) -> p b o", b=16),
                d_wob[:, (q - 2) * 512:(q - 1) * 512]
                .rearrange("(b p) o -> p b o", p=128)),
                waits=[("T", o_state[("q", q - 2)]), ("dcwo", ev_dcwo)],
                inc16=f"dwo{q}")

        tp_after_rc = {}
        ev_r = {}

        def tail_apply(rc):
            slot = rc % 2
            ro, no = 32 + rc * 2, 33 + rc * 2
            ev_ap2 = p.op("V", lambda e, sl=slot, ro=ro, no=no,
                          xo=B(XNT + slot * 2048, 2048): e.tensor_scalar(
                xo, XB[sl][:, :], ST[:, ro:ro + 1], ST[:, no:no + 1],
                ALU.mult, ALU.add),
                waits=[(f"dx{slot}", ev_r[rc]), xns_free[slot], ("S", sP3)],
                inc="V")
            xb_free[slot] = ("V", ev_ap2)
            tp_cnt[slot] += 1
            p.op("Y", lambda e, sl=slot, t=rc: e.dma_start(
                B(XT_P4, 16384).rearrange("p (b t) -> p b t", b=16)[:, :, t * 128:(t + 1) * 128],
                B(XNT + sl * 2048, 2048).rearrange("p (b h) -> p b h", b=16),
                transpose=True), waits=[("V", ev_ap2), ("T", ev_lastQK)],
                inc16=f"dtp{slot}")
            xns_free[slot] = (f"dtp{slot}", tp_cnt[slot] * 16)
            if rc in (3, 7):
                tp_after_rc[rc] = (tp_cnt[0] * 16, tp_cnt[1] * 16)

        ev_r[0] = load_x(d_xo, 0 * 128, 0, ring="S")
        ev_r[1] = load_x(d_xo, 1 * 128, 1, ring="S")
        tail_apply(0)
        tail_apply(1)
        ev_r[2] = load_x(d_xo, 2 * 128, 0, ring="S")
        ev_r[3] = load_x(d_xo, 3 * 128, 1, ring="S")
        for tl in range(4):
            o_group(0, 1, tl, [0, 1, 6, 7], eng="V")
        wo_late_load(2, "Y")
        tail_apply(2)
        tail_apply(3)
        ev_r[4] = load_x(d_xo, 4 * 128, 0, ring="S")
        for tl in range(4):
            o_group(1, 1, tl, [0, 1, 6, 7], eng="V")
        wo_late_load(3, "S")
        tail_apply(4)
        ev_r[5] = load_x(d_xo, 5 * 128, 1, ring="S")
        for tl in range(4):
            o_group(2, 0, tl, [0, 1, 6, 7], eng="V")
        tail_apply(5)
        ev_r[6] = load_x(d_xo, 6 * 128, 0, ring="S")
        for tl in range(4):
            o_group(2, 1, tl, [0, 1, 6, 7], eng="V")
        ev_r[7] = load_x(d_xo, 7 * 128, 1, ring="S")
        for tl in range(4):
            o_group(3, 0, tl, [0, 1, 6, 7], eng="V")
        tail_apply(6)
        for tl in range(4):
            o_group(3, 1, tl, [0, 1, 6, 7], eng="V")
        tail_apply(7)
        ev_O_T_all = o_state["last_T"]
        gu_slot_free = {i: None for i in range(4)}
        gu_cnt = [0, 0, 0, 0]
        ev_gu_last = None
        sg_evs = {}
        sg_free = {0: None, 1: None}
        gu_mm_ev = {}

        def gu_transpose(ic):
            for wt in range(2):
                slot = wt + 2 * (ic % 2)
                base = GUW[slot]
                srcw = d_gwb if wt == 0 else d_uwb
                w_tp = [(("dcg", ev_dcg) if wt == 0 else ("dcu", ev_dcu)),
                        gu_slot_free[slot]]
                if ic < 2 and slot < 2:
                    w_tp.append(("T", ev_O_T_all))
                if ic == 2:
                    w_tp += [("dgu2", p.cnt["dgu2"]), ("dgu3", p.cnt["dgu3"])]
                gu_cnt[slot] += 16
                p.op("Y" if ic < 2 else "S",
                     lambda e, b=base, s=srcw, r=ic * 128: e.dma_start(
                    B(b, 2048).rearrange("p (m q) -> p m q", m=16),
                    s[r:r + 128, :].rearrange("p (m q) -> p m q", m=16),
                    transpose=True), waits=w_tp, inc16=f"dgu{slot}")

        gu_transpose(0)
        gu_transpose(1)
        for rc in range(8):
            slot = rc % 2
            ev_x2 = load_x(d_xo, rc * 128, slot, ring="S")
            ev_xa = p.op("V", lambda e, rc=rc, sl=slot: e.tensor_tensor(
                B(AO + rc * 2048, 2048), B(AO + rc * 2048, 2048), XB[sl][:, :],
                ALU.add), waits=[(f"dx{slot}", ev_x2), ev_O_evac_tc[rc]], inc="V")
            xb_free[slot] = ("V", ev_xa)
        ev_tp0_all, ev_tp1_all = tp_cnt[0] * 16, tp_cnt[1] * 16

        for ic in range(NIC):
            wtev = {}
            for wt in range(2):
                slot = wt + 2 * (ic % 2)
                base = GUW[slot]
                evs = []
                for tb in range(2):
                    bank = wt * 2 + tb
                    t0, t1 = tp_after_rc[3 if tb == 0 else 7]
                    w0 = [(f"dgu{slot}", gu_cnt[slot]), ("dtp0", t0), ("dtp1", t1)]
                    tiles = [(B(base + hc * 128, 128),
                              B(XT_P4 + hc * 1024 + tb * 512, 512)) for hc in range(NHC)]
                    evt = mm_group(bank, 512, tiles, w0)
                    evs.append(evt)
                    wtev[(wt, tb)] = evt
                    ev_gu_last = evt
                if wt == 1:
                    gu_mm_ev[ic] = evs[-1]
                    gu_slot_free[0 + 2 * (ic % 2)] = ("T", evs[-1])
                    gu_slot_free[1 + 2 * (ic % 2)] = ("T", evs[-1])
            for tb in range(2):
                ev_si = evac("S", 0 + tb,
                    lambda e, b=0 + tb, c=ic: e.activation(
                        B(PS0 + (b % 2) * 512, 512), pb[b][:, :], AF.Silu,
                        scale=GSC[:, c:c + 1]),
                    waits=[("T", wtev[(0, tb)]), ("dwc", ev_dwc), sg_free[tb]])
                sg_evs[tb] = ("S", ev_si)
            for tb in range(2):
                ev_h = evac("V", 2 + tb,
                    lambda e, b=2 + tb, c=ic, tb=tb: e.scalar_tensor_tensor(
                        B(hdn_base(c) + tb * 512, 512), pb[b][:, :], USC[:, c:c + 1],
                        B(PS0 + tb * 512, 512), ALU.mult, ALU.mult),
                    waits=[("T", wtev[(1, tb)]), sg_evs[tb], ("dwc", ev_dwc)])
                sg_free[tb] = ("V", ev_h)
            if ic + 2 < NIC:
                gu_transpose(ic + 2)
            if stop_after == "p4ic2" and ic == 1:
                do_taps([("dbg_guw", GUW[0], 2048), ("dbg_guw1", GUW[1], 2048),
                         ("dbg_guw2", GUW[2], 2048), ("dbg_guw3", GUW[3], 2048),
                         ("dbg_h01", 0, 2048), ("dbg_xto3", XT_OWN, 16384)])
                break

        if stop_after == "p4gu":
            do_taps([("dbg_hdnA", 0, 16384), ("dbg_hdnB", 49152, 16384),
                     ("dbg_hdnC", 65536, 12288)])
        dw_slot_free = {0: None, 1: None}
        dw_cnt = [0, 0]
        ev_down_last_T = None
        for hb in range(8):
            dsl = hb % 2
            w_tp = [("dcd", ev_dcd), ("T", ev_gu_last), dw_slot_free[dsl]]
            for hcs in range(2):
                hc = hb * 2 + hcs
                if dsl == 0:
                    dw_cnt[dsl] += 16
                    p.op("S", lambda e, hc=hc, par=hcs: e.dma_start(
                        B(DWA, 11264).rearrange("p (m q) -> p m q", m=NIC)[:, :, par * 128:par * 128 + 128],
                        d_dwb[hc * 128:(hc + 1) * 128, :].rearrange("p (m q) -> p m q", m=NIC),
                        transpose=True), waits=(w_tp if hcs == 0 else ()), inc16=f"ddw{dsl}")
                else:
                    for pi, (pbase, iclo, nic) in enumerate(DWB_PIECES):
                        dw_cnt[dsl] += 16
                        p.op("S", lambda e, hc=hc, par=hcs, b=pbase, lo=iclo, n=nic:
                             e.dma_start(
                                 B(b, n * 256).rearrange("p (m q) -> p m q", m=n)[:, :, par * 128:par * 128 + 128],
                                 d_dwb[hc * 128:(hc + 1) * 128, lo * 128:(lo + n) * 128]
                                 .rearrange("p (m q) -> p m q", m=n),
                                 transpose=True),
                             waits=(w_tp if (hcs == 0 and pi == 0) else ()), inc16=f"ddw{dsl}")
            ev_ddw = dw_cnt[dsl]
            stage = XB[dsl]
            for tl in range(8):
                bank = 4 + (hb * 8 + tl) % 4
                if dsl == 0:
                    tiles = [(B(hdn_base(ic) + tl * 128, 128),
                              B(DWA + ic * 256, 256)) for ic in range(NIC)]
                else:
                    tiles = [(B(hdn_base(ic) + tl * 128, 128),
                              B(dwb_tile(ic), 256)) for ic in range(NIC)]
                evt = mm_group(bank, 256, tiles, [(f"ddw{dsl}", ev_ddw)])
                ev_down_last_T = evt
                dw_slot_free[dsl] = ("T", evt)
                p.op("V", lambda e, b=bank, st=stage, tl=tl, hb=hb: e.tensor_tensor(
                    st[:, tl * 256:(tl + 1) * 256], pb[b][:, 0:256],
                    B(DS + hb * 256, 256), ALU.mult),
                    waits=[("T", evt), ("dwc", ev_dwc), xb_free[dsl]])
                ev_e2 = evac("V", bank,
                    lambda e, st=stage, tl=tl, hb=hb: e.tensor_tensor(
                        st[:, tl * 256:(tl + 1) * 256], st[:, tl * 256:(tl + 1) * 256],
                        B(AO + tl * 2048 + hb * 256, 256), ALU.add),
                    waits=[])
                p.op("Y", lambda e, st=stage, tl=tl, hb=hb: e.dma_start(
                    d_out[tl * 128:(tl + 1) * 128, hb * 256:(hb + 1) * 256],
                    st[:, tl * 256:(tl + 1) * 256]),
                    waits=[("V", ev_e2)], inc16="dout")
            xb_free[dsl] = ("dout", p.cnt["dout"])

        p.op("Y", lambda e: e.wait_ge(sems["dout"], p.cnt["dout"]))

        with nc.Block() as blk:
            def mk_body(lst):
                def body(eng):
                    for fn, w, inc, inc16 in lst:
                        for s, v in w:
                            eng.wait_ge(sems[s], v)
                        inst = fn(eng)
                        if inc:
                            inst.then_inc(sems[inc], 1)
                        if inc16:
                            inst.then_inc(sems[inc16], 16)
                return body
            blk.tensor(mk_body(p.ops["T"]))
            blk.vector(mk_body(p.ops["V"]))
            blk.scalar(mk_body(p.ops["S"]))
            blk.sync(mk_body(p.ops["Y"]))
            blk.gpsimd(mk_body(p.ops["G"]))
    return nc


def kernel(**inputs):
    f32 = np.float32
    x = np.ascontiguousarray(np.asarray(inputs["L0_attn_hidden_states"]).reshape(8192, 2048))
    wq = np.ascontiguousarray(inputs["L0_attn_Wq"])
    wk = np.ascontiguousarray(inputs["L0_attn_Wk"])
    wv = np.ascontiguousarray(inputs["L0_attn_Wv"])
    wo = np.ascontiguousarray(inputs["L0_attn_Wo"])
    gw = np.ascontiguousarray(inputs["L0_ffn_gate_w"])
    uw = np.ascontiguousarray(inputs["L0_ffn_up_w"])
    dw = np.ascontiguousarray(inputs["L0_ffn_down_w"])
    gs = np.ascontiguousarray(np.asarray(inputs["L0_ffn_gate_s"]).reshape(NIC, 128).T)
    us = np.ascontiguousarray(np.asarray(inputs["L0_ffn_up_s"]).reshape(NIC, 128).T)
    ds = np.ascontiguousarray(np.broadcast_to(
        np.asarray(inputs["L0_ffn_down_s"])[None, :], (128, 2048)))
    kr = np.arange(128)[:, None]
    qc = np.arange(128)[None, :]
    msk = np.ascontiguousarray(np.concatenate([
        np.where(qc <= kr, 0.0, -1e30),
        np.where(kr <= qc, 0.0, -1e30),
    ], axis=1).astype(f32))
    ones = np.ones((128, 128), f32)
    zeros_ph = np.zeros((128, 1024), f32)
    ph0 = np.ascontiguousarray(np.broadcast_to(
        (np.arange(1024) - 1024.0).astype(f32)[None, :], (128, 1024)))

    nc = build_program()
    in_maps = []
    for c in range(8):
        x_own = np.ascontiguousarray(x[c * TOK:(c + 1) * TOK])
        x_prev = (np.ascontiguousarray(x[(c - 1) * TOK:c * TOK]) if c > 0
                  else np.zeros((TOK, H), f32))
        in_maps.append(dict(
            x_own=x_own, x_prev=x_prev,
            wq=wq, wk=wk, wv=wv, wo=wo, gate_w=gw, up_w=uw, down_w=dw,
            gate_s=gs, up_s=us, down_s_rep=ds, masks=msk,
            phantom=(ph0 if c == 0 else zeros_ph), ones_in=ones))
    res = run_bass_kernel_spmd(nc, in_maps, core_ids=list(range(8)))
    out = np.concatenate([r["out"] for r in res.results], axis=0)
    return out.reshape(1, 8192, 2048).astype(np.float32, copy=False)


# version 44 (final)
# speedup vs baseline: 1.0346x; 1.0052x over previous
import contextlib
import numpy as np

import concourse.bass as bass
import concourse.mybir as mybir
from concourse.bass_utils import run_bass_kernel_spmd

F32 = mybir.dt.float32
BF16 = mybir.dt.bfloat16
AF = mybir.ActivationFunctionType
ALU = mybir.AluOpType
AX = mybir.AxisListType

TOK = 1024
H = 2048
NHC = 16
NH = 16
NKV = 4
INTER = 5632
NIC = 44
SC = 0.08838834764831845
EPS = 1e-5

NBIG = 90624
R0 = 0
XT_OWN = 16384
WK = 32768
WV = 40960
AO = 32768
KS = 49152
VS = 57344
QS = 65536
PS0 = 81920
ONES = 82944
XNT = 84480
DS = 88576
GUW = (28672, 30720, 84480, 86528)
XT_P4 = 65536
DWA = 65536
DWB_PIECES = ((76800, 0, 20), (84480, 20, 16), (28672, 36, 8))


def hdn_base(ic):
    if ic < 16:
        return ic * 1024
    if ic < 32:
        return 49152 + (ic - 16) * 1024
    return 16384 + (ic - 32) * 1024


def dwb_tile(ic):
    for base, lo, n in DWB_PIECES:
        if lo <= ic < lo + n:
            return base + (ic - lo) * 256
    raise AssertionError

SEMS = ["T", "V", "S", "dx0", "dx1", "dtp0", "dtp1", "dwc", "dwv", "dwk",
        "dwq0", "dwq1", "dwq2", "dwq3", "dwo0", "dwo1", "dwo2", "dwo3",
        "dcwo", "dcg", "dcu", "dcd",
        "dgu0", "dgu1", "dgu2", "dgu3", "ddw0", "ddw1", "dout"]


class P:
    def __init__(self):
        self.cnt = {s: 0 for s in SEMS}
        self.ops = {e: [] for e in "TVSYG"}
        self.lastw = {e: {} for e in "TVSYG"}

    def op(self, eng, fn, waits=(), inc=None, inc16=None):
        w = []
        for item in waits:
            if item is None:
                continue
            s, v = item
            if v is None or v <= 0:
                continue
            if self.lastw[eng].get(s, -1) >= v:
                continue
            w.append((s, v))
            self.lastw[eng][s] = v
        if inc:
            self.cnt[inc] += 1
        if inc16:
            self.cnt[inc16] += 16
        self.ops[eng].append((fn, w, inc, inc16))
        return self.cnt[inc] if inc else (self.cnt[inc16] if inc16 else None)


def build_program(stop_after=None):
    nc = bass.Bass("TRN2", target_bir_lowering=False)
    ap = lambda t: t.ap()
    d_xo = ap(nc.dram_tensor("x_own", [TOK, H], F32, kind="ExternalInput"))
    d_xp = ap(nc.dram_tensor("x_prev", [TOK, H], F32, kind="ExternalInput"))
    d_wq = ap(nc.dram_tensor("wq", [H, H], F32, kind="ExternalInput"))
    d_wk = ap(nc.dram_tensor("wk", [H, 512], F32, kind="ExternalInput"))
    d_wv = ap(nc.dram_tensor("wv", [H, 512], F32, kind="ExternalInput"))
    d_wo = ap(nc.dram_tensor("wo", [H, H], F32, kind="ExternalInput"))
    d_gw = ap(nc.dram_tensor("gate_w", [INTER, H], F32, kind="ExternalInput"))
    d_uw = ap(nc.dram_tensor("up_w", [INTER, H], F32, kind="ExternalInput"))
    d_dw = ap(nc.dram_tensor("down_w", [H, INTER], F32, kind="ExternalInput"))
    d_gs = ap(nc.dram_tensor("gate_s", [128, NIC], F32, kind="ExternalInput"))
    d_us = ap(nc.dram_tensor("up_s", [128, NIC], F32, kind="ExternalInput"))
    d_ds = ap(nc.dram_tensor("down_s_rep", [128, H], F32, kind="ExternalInput"))
    d_msk = ap(nc.dram_tensor("masks", [128, 256], F32, kind="ExternalInput"))
    d_ph = ap(nc.dram_tensor("phantom", [128, 1024], F32, kind="ExternalInput"))
    d_ones = ap(nc.dram_tensor("ones_in", [128, 128], F32, kind="ExternalInput"))
    d_out = ap(nc.dram_tensor("out", [TOK, H], F32, kind="ExternalOutput"))
    d_gwb = ap(nc.dram_tensor("gw_bf", [INTER, H], BF16, kind="Internal"))
    d_uwb = ap(nc.dram_tensor("uw_bf", [INTER, H], BF16, kind="Internal"))
    d_dwb = ap(nc.dram_tensor("dw_bf", [H, INTER], BF16, kind="Internal"))
    d_wob = ap(nc.dram_tensor("wo_bf", [H, 1024], BF16, kind="Internal"))

    p = P()
    with contextlib.ExitStack() as ctx:
        sems = {s: ctx.enter_context(nc.semaphore(s)) for s in SEMS}
        BIG = ctx.enter_context(nc.sbuf_tensor("BIG", [128, NBIG], BF16))
        XB = [ctx.enter_context(nc.sbuf_tensor(f"xb{i}", [128, H], F32)) for i in range(2)]
        FAR = ctx.enter_context(nc.sbuf_tensor("far", [128, 1024], F32))
        ST = ctx.enter_context(nc.sbuf_tensor("stats", [128, 64], F32))
        MSK = ctx.enter_context(nc.sbuf_tensor("msk", [128, 256], F32))
        PH = ctx.enter_context(nc.sbuf_tensor("ph", [128, 1024], F32))
        GSC = ctx.enter_context(nc.sbuf_tensor("gsc", [128, NIC], F32))
        USC = ctx.enter_context(nc.sbuf_tensor("usc", [128, NIC], F32))
        pb = [ctx.enter_context(nc.psum_tensor(f"pb{i}", [128, 512], F32)) for i in range(8)]

        B = lambda a, n: BIG[:, a:a + n]

        bank_free = {i: None for i in range(8)}
        xb_free = [None, None]
        xb_cnt = [0, 0]
        tp_cnt = [0, 0]
        XNS = (XNT, XNT + 2048, 65536, 67584)
        xns_free = {i: None for i in range(4)}

        def load_x(src_ap, rows, slot, ring="Y"):
            xb_cnt[slot] += 1
            p.op(ring, lambda e, s=src_ap, r=rows, sl=slot:
                 e.dma_start(XB[sl][:, :], s[r:r + 128, :]),
                 waits=[xb_free[slot]], inc16=f"dx{slot}")
            return xb_cnt[slot] * 16

        p.op("G", lambda e: e.dma_start(B(ONES, 128), d_ones), inc16="dwc")
        p.op("G", lambda e: e.dma_start(B(DS, H), d_ds), inc16="dwc")
        p.op("Y", lambda e: e.dma_start(MSK[:, :], d_msk), inc16="dwc")
        p.op("Y", lambda e: e.dma_start(PH[:, :], d_ph), inc16="dwc")
        p.op("Y", lambda e: e.dma_start(GSC[:, :], d_gs), inc16="dwc")
        p.op("Y", lambda e: e.dma_start(USC[:, :], d_us), inc16="dwc")
        ev_dwc = p.cnt["dwc"]
        ev_wvk = {}
        for dst, src2, sem in ((WV, d_wv, "dwv"), (WK, d_wk, "dwk")):
            p.op("G", lambda e, d=dst, s=src2: e.dma_start(
                B(d, 8192).rearrange("p (b o) -> p b o", b=16),
                s.rearrange("(b p) o -> p b o", p=128)), inc16=sem)
            ev_wvk[sem] = p.cnt[sem]
        def precast(dst, srcw, rows, sem):
            qr = rows // 4
            for i in range(4):
                p.op("G", lambda e, d=dst, s=srcw, a=i * qr, b2=(i + 1) * qr:
                     e.dma_start(d[a:b2, :], s[a:b2, :]), inc16=sem)
            return p.cnt[sem]

        def ln_pair(specs, extra_s=(), extra_y=(), ring="Y"):
            n = len(specs)
            evx, evsq, xnt, ev = [None] * n, [None] * n, [None] * n, [None] * n
            xns = [sp[5] if len(sp) > 5 else sp[2] for sp in specs]
            for i, sp in enumerate(specs):
                evx[i] = load_x(sp[0], sp[1], sp[2], ring=ring)
            for i, sp in enumerate(specs):
                slot, c = sp[2], sp[2] * 8
                ev[i] = p.op("V", lambda e, sl=slot, c=c: e.reduce_sum(
                    ST[:, c:c + 1], XB[sl][:, :], AX.X),
                    waits=[(f"dx{slot}", evx[i])], inc="V")
            for i, sp in enumerate(specs):
                slot, c = sp[2], sp[2] * 8
                xnt[i] = B(XNS[xns[i]], 2048)
                evsq[i] = p.op("S", lambda e, sl=slot, c=c, xo=xnt[i]: e.activation(
                    xo, XB[sl][:, :], AF.Square, accum_out=ST[:, c + 1:c + 2]),
                    waits=[(f"dx{slot}", evx[i]), xns_free[xns[i]]]
                    + list(extra_s), inc="S")
            def step(i, fn, waits=()):
                slot = specs[i][2]
                ev[i] = p.op("V", fn(slot * 8), waits=[("V", ev[i])] + list(waits), inc="V")
            for i in range(n):
                step(i, lambda c: lambda e: e.tensor_scalar_mul(
                    ST[:, c + 2:c + 3], ST[:, c:c + 1], 1.0 / H))
            for i in range(n):
                step(i, lambda c: lambda e: e.tensor_scalar_mul(
                    ST[:, c + 3:c + 4], ST[:, c + 1:c + 2], 1.0 / H),
                    waits=[("S", evsq[i])])
            for i in range(n):
                step(i, lambda c: lambda e: e.tensor_mul(
                    ST[:, c + 4:c + 5], ST[:, c + 2:c + 3], ST[:, c + 2:c + 3]))
            for i in range(n):
                step(i, lambda c: lambda e: e.scalar_tensor_tensor(
                    ST[:, c + 4:c + 5], ST[:, c + 3:c + 4], EPS, ST[:, c + 4:c + 5],
                    ALU.add, ALU.subtract))
            evsd = [None] * n
            for i in range(n):
                c = specs[i][2] * 8
                evsd[i] = p.op("S", lambda e, c=c: e.activation(
                    ST[:, c + 5:c + 6], ST[:, c + 4:c + 5], AF.Sqrt),
                    waits=[("V", ev[i])], inc="S")
            for i in range(n):
                step(i, lambda c: lambda e: e.reciprocal(
                    ST[:, c + 6:c + 7], ST[:, c + 5:c + 6]), waits=[("S", evsd[i])])
            for i in range(n):
                step(i, lambda c: lambda e: e.scalar_tensor_tensor(
                    ST[:, c + 7:c + 8], ST[:, c + 2:c + 3], -1.0, ST[:, c + 6:c + 7],
                    ALU.mult, ALU.mult))
            for i, sp in enumerate(specs):
                src_ap, rows, slot, dest_base, tcl = sp[:5]
                c = slot * 8
                if i % 2 == 0:
                    ev_ap2 = p.op("V", lambda e, sl=slot, c=c, xo=xnt[i]: e.tensor_scalar(
                        xo, XB[sl][:, :], ST[:, c + 6:c + 7], ST[:, c + 7:c + 8],
                        ALU.mult, ALU.add), waits=[("V", ev[i])], inc="V")
                    xb_free[slot] = ("V", ev_ap2)
                    apw = ("V", ev_ap2)
                else:
                    ev_ap2 = p.op("S", lambda e, sl=slot, c=c, xo=xnt[i]: e.activation(
                        xo, XB[sl][:, :], AF.Identity, bias=ST[:, c + 7:c + 8],
                        scale=ST[:, c + 6:c + 7]), waits=[("V", ev[i])], inc="S")
                    xb_free[slot] = ("S", ev_ap2)
                    apw = ("S", ev_ap2)
                tp_cnt[slot] += 1
                p.op("Y", lambda e, xo=xnt[i], db=dest_base, t=tcl: e.dma_start(
                    B(db, 16384).rearrange("p (b t) -> p b t", b=16)[:, :, t * 128:(t + 1) * 128],
                    xo.rearrange("p (b h) -> p b h", b=16),
                    transpose=True), waits=[apw] + list(extra_y),
                    inc16=f"dtp{slot}")
                xns_free[xns[i]] = (f"dtp{slot}", tp_cnt[slot] * 16)

        def do_taps(names_bases):
            dsems = [s for s in SEMS if s not in ("T", "V", "S")]
            w = ([("T", p.cnt["T"]), ("V", p.cnt["V"]), ("S", p.cnt["S"])] +
                 [(s, p.cnt[s]) for s in dsems])
            for nm, base, ncols in names_bases:
                d = ap(nc.dram_tensor(nm, [128, ncols], BF16, kind="ExternalOutput"))
                p.op("Y", lambda e, d=d, b=base, n=ncols: e.dma_start(d, B(b, n)),
                     waits=w, inc16="dout")
                w = []
            vfin = p.cnt["dout"]
            for eng in ("Y", "T", "V", "S", "G"):
                p.op(eng, lambda e, v=vfin: e.wait_ge(sems["dout"], v))

        def p1_spec(tc):
            s, rows = (d_xp, tc * 128) if tc < 8 else (d_xo, (tc - 8) * 128)
            base, tcl = (R0, tc) if tc < 8 else (XT_OWN, tc - 8)
            return (s, rows, tc % 2, base, tcl, tc % 4)
        for pr in range(8):
            ln_pair([p1_spec(2 * pr), p1_spec(2 * pr + 1)], ring="S")
        ev_tp0_p1, ev_tp1_p1 = tp_cnt[0] * 16, tp_cnt[1] * 16
        if stop_after == "p1":
            do_taps([("dbg_xtp", R0, 16384), ("dbg_xto", XT_OWN, 16384)])

        xt_tile = lambda hc, tb: B((R0 if tb < 2 else XT_OWN) + hc * 1024 + (tb % 2) * 512, 512)
        xt_ttile = lambda hc, tc: B((R0 if tc < 8 else XT_OWN) + hc * 1024 + (tc % 8) * 128, 128)

        def mm_group(bank, ncols, tiles, waits_first):
            w0 = list(waits_first)
            if bank_free[bank]:
                w0.append(bank_free[bank])
            n = len(tiles)
            ev = None
            for i, (lt, rt) in enumerate(tiles):
                ev = p.op("T", lambda e, l=lt, r=rt, st=(i == 0), sp=(i == n - 1),
                          b=bank, nc_=ncols: e.matmul(pb[b][:, 0:nc_], l, r, start=st, stop=sp),
                          waits=(w0 if i == 0 else ()), inc=("T" if i == n - 1 else None))
            return ev

        def evac(eng, bank, fn, waits):
            ev = p.op(eng, fn, waits=waits, inc=eng)
            bank_free[bank] = (eng, ev)
            return ev

        grp = 0
        ev_prev_T = None

        def tp_waits(chunks):
            n0 = max((ch // 2 + 1 for ch in chunks if ch % 2 == 0), default=0)
            n1 = max((ch // 2 + 1 for ch in chunks if ch % 2 == 1), default=0)
            return [("dtp0", n0 * 16), ("dtp1", n1 * 16)]

        def k_group(kv, tb, bank=None, eng="S"):
            nonlocal grp
            bank = (grp % 4) if bank is None else bank
            tiles = [(B(WK + hc * 512 + kv * 128, 128), xt_tile(hc, tb)) for hc in range(NHC)]
            evt = mm_group(bank, 512, tiles, tp_waits(range(tb * 4, tb * 4 + 4))
                           + [("dwk", ev_wvk["dwk"])])
            if eng == "V":
                fn = (lambda e, b=bank, d=KS + kv * 2048 + tb * 512:
                      e.tensor_scalar_mul(B(d, 512), pb[b][:, :], SC))
            else:
                fn = (lambda e, b=bank, d=KS + kv * 2048 + tb * 512:
                      e.activation(B(d, 512), pb[b][:, :], AF.Identity, scale=SC))
            evac(eng, bank, fn, waits=[("T", evt)])
            grp += 1
            return evt

        def v_group(tc, bank=None, eng="S"):
            nonlocal grp
            bank = (grp % 4) if bank is None else bank
            tiles = [(xt_ttile(hc, tc), B(WV + hc * 512, 512)) for hc in range(NHC)]
            evt = mm_group(bank, 512, tiles, tp_waits([tc]) + [("dwv", ev_wvk["dwv"])])
            if eng == "V":
                fn = (lambda e, b=bank, d=VS + tc * 512:
                      e.tensor_copy(B(d, 512), pb[b][:, :]))
            else:
                fn = (lambda e, b=bank, d=VS + tc * 512:
                      e.copy(B(d, 512), pb[b][:, :]))
            evac(eng, bank, fn, waits=[("T", evt)])
            grp += 1
            return evt

        for tc in range(8):
            v_group(tc)
        for tb in range(2):
            for kv in range(NKV):
                ev_prev_T = k_group(kv, tb)
        for q in range(2):
            p.op("G", lambda e, q=q: e.dma_start(
                B(R0 + q * 8192, 8192).rearrange("p (b o) -> p b o", b=16),
                d_wq[:, q * 512:(q + 1) * 512].rearrange("(b p) o -> p b o", p=128)),
                waits=[("T", ev_prev_T)], inc16=f"dwq{q}")
        for tc in range(8, 12):
            v_group(tc)
        for kv in range(NKV):
            k_group(kv, 2)
        ev_q_half_last = None
        ev_q_evac = {}
        ev_q_quarter = {0: None, 1: None}
        for q in range(4):
            if q >= 2:
                p.op("G", lambda e, q=q: e.dma_start(
                    B(R0 + (q % 2) * 8192, 8192).rearrange("p (b o) -> p b o", b=16),
                    d_wq[:, q * 512:(q + 1) * 512].rearrange("(b p) o -> p b o", p=128)),
                    waits=[("T", ev_q_quarter[q % 2])], inc16=f"dwq{q}")
            for dloc2 in range(4):
                dloc = q * 4 + dloc2
                for tb in range(2):
                    bank = grp % 4
                    tiles = [(B(R0 + (q % 2) * 8192 + hc * 512 + dloc2 * 128, 128),
                              B(XT_OWN + hc * 1024 + tb * 512, 512)) for hc in range(NHC)]
                    evt = mm_group(bank, 512, tiles,
                                   [(f"dwq{q}", 16), ("dtp0", tp_cnt[0] * 16),
                                    ("dtp1", tp_cnt[1] * 16)])
                    ev_q_half_last = evt
                    ev_q_quarter[q % 2] = evt
                    ev_q_evac[dloc] = evac(
                        "S", bank, lambda e, b=bank, d=QS + dloc * 1024 + tb * 512:
                        e.copy(B(d, 512), pb[b][:, :]), waits=[("T", evt)])
                    grp += 1
        ev_dcg = precast(d_gwb, d_gw, INTER, "dcg")
        for q in range(2):
            p.op("G", lambda e, q=q: e.dma_start(
                B(R0 + q * 8192, 8192).rearrange("p (b o) -> p b o", b=16),
                d_wo[:, q * 512:(q + 1) * 512].rearrange("(b p) o -> p b o", p=128)),
                waits=[("T", ev_q_half_last)], inc16=f"dwo{q}")
        for i in range(4):
            p.op("G", lambda e, a=i * 512, b2=(i + 1) * 512: e.dma_start(
                d_wob[a:b2, :], d_wo[a:b2, 1024:2048]), inc16="dcwo")
        ev_dcwo = p.cnt["dcwo"]
        ev_dcu = precast(d_uwb, d_uw, INTER, "dcu")
        ev_dcd = precast(d_dwb, d_dw, H, "dcd")
        ev_S_P2 = p.cnt["S"]
        if stop_after == "p2":
            do_taps([("dbg_ks", KS, 8192), ("dbg_vs", VS, 8192), ("dbg_qs", QS, 16384)])

        ln_prev_ms = [None]

        ln_pending = {}

        def ln_stats_load(rc):
            ln_pending[rc] = load_x(d_xo, rc * 128, rc % 2, ring="Y")

        def ln_stats_chunk(src_ap, rows, slot, rc):
            c, ro, no = 16, 32 + rc * 2, 33 + rc * 2
            ev_x = ln_pending[rc]
            ev1 = p.op("V", lambda e, sl=slot, c=c: e.reduce_sum(
                ST[:, c:c + 1], XB[sl][:, :], AX.X), waits=[(f"dx{slot}", ev_x)], inc="V")
            ev_sq = p.op("S", lambda e, sl=slot, c=c, xo=B(XNT + slot * 2048, 2048):
                e.activation(xo, XB[sl][:, :], AF.Square, accum_out=ST[:, c + 1:c + 2]),
                waits=[(f"dx{slot}", ev_x), xns_free[slot], ln_prev_ms[0],
                       ("V", ev1)], inc="S")
            xb_free[slot] = ("S", ev_sq)
            ev1 = p.op("V", lambda e, c=c: e.tensor_scalar_mul(
                ST[:, c + 2:c + 3], ST[:, c:c + 1], 1.0 / H), waits=[("V", ev1)], inc="V")
            ev_ms = p.op("V", lambda e, c=c: e.tensor_scalar_mul(
                ST[:, c + 3:c + 4], ST[:, c + 1:c + 2], 1.0 / H),
                waits=[("S", ev_sq), ("V", ev1)], inc="V")
            ln_prev_ms[0] = ("V", ev_ms)
            ev1 = p.op("V", lambda e, c=c: e.tensor_mul(
                ST[:, c + 4:c + 5], ST[:, c + 2:c + 3], ST[:, c + 2:c + 3]),
                waits=[("V", ev_ms)], inc="V")
            ev1 = p.op("V", lambda e, c=c: e.scalar_tensor_tensor(
                ST[:, c + 4:c + 5], ST[:, c + 3:c + 4], EPS, ST[:, c + 4:c + 5],
                ALU.add, ALU.subtract), waits=[("V", ev1)], inc="V")
            ev_sd = p.op("S", lambda e, c=c: e.activation(
                ST[:, c + 5:c + 6], ST[:, c + 4:c + 5], AF.Ln),
                waits=[("V", ev1)], inc="S")
            ev_sd = p.op("S", lambda e, c=c, ro=ro: e.activation(
                ST[:, ro:ro + 1], ST[:, c + 5:c + 6], AF.Exp, scale=-0.5),
                waits=[("S", ev_sd)], inc="S")
            p.op("V", lambda e, c=c, ro=ro, no=no: e.scalar_tensor_tensor(
                ST[:, no:no + 1], ST[:, c + 2:c + 3], -1.0, ST[:, ro:ro + 1],
                ALU.mult, ALU.mult), waits=[("S", ev_sd)], inc="V")

        ev_O_evac_tc = {}
        o_state = {"last_T": None}
        o_rr = {"i": 0}

        def o_group(q, g, tl, banks, eng="S"):
            bank = banks[o_rr["i"] % len(banks)]
            o_rr["i"] += 1
            w0 = [(f"dwo{q}", 16), ("V", ev_attnevac[g * 16 + 15])]
            tiles = [(B(XT_OWN + g * 8192 + hc * 512 + tl * 128, 128),
                      B(R0 + (q % 2) * 8192 + hc * 512, 512)) for hc in range(NHC)]
            evt = mm_group(bank, 512, tiles, w0)
            o_state["last_T"] = evt
            o_state[("q", q)] = evt
            if eng == "V":
                fn = (lambda e, b=bank, d=AO + (g * 4 + tl) * 2048 + q * 512:
                      e.tensor_copy(B(d, 512), pb[b][:, :]))
            else:
                fn = (lambda e, b=bank, d=AO + (g * 4 + tl) * 2048 + q * 512:
                      e.copy(B(d, 512), pb[b][:, :]))
            ev_O_evac_tc[g * 4 + tl] = (eng, evac(eng, bank, fn, waits=[("T", evt)]))

        ev_attnevac = {}
        ev_ln = {}
        ev_lastQK = None
        ev_late_kv = [0]
        for g in range(2):
            for h in range(NH):
                hh = g * 16 + h
                kv = h // 4
                ab, lb = 2 + h % 2, 4 + h % 2
                ev_QK, ev_exp, ev_l, ev_msk = {}, {}, {}, {}

                def emit_qk(j, g=g, h=h, kv=kv):
                    kt = 4 * g + j
                    c0, c1 = max(0, j - 8) * 128, (min(3, j) + 1) * 128
                    w_qk = [("S", ev_q_evac[h])]
                    if g == 1 and j >= 8:
                        w_qk.append(("V", ev_late_kv[0]))
                    ev_QK[j] = p.op("T", lambda e, j=j, kt=kt, c0=c0, c1=c1, kv=kv, g=g, h=h:
                                    e.matmul(pb[j % 2][:, c0:c1],
                                             B(KS + kv * 2048 + kt * 128, 128),
                                             B(QS + h * 1024 + g * 512 + c0, c1 - c0),
                                             start=True, stop=True),
                                    waits=w_qk, inc="T")
                    if j <= 3 or j >= 8:
                        mc = 0 if j <= 3 else 128
                        blk = (j if j <= 3 else j - 8) * 128
                        ev_msk[j] = p.op("V", lambda e, j=j, blk=blk, mc=mc: e.tensor_tensor(
                            pb[j % 2][:, blk:blk + 128], pb[j % 2][:, blk:blk + 128],
                            MSK[:, mc:mc + 128], ALU.add),
                            waits=[("T", ev_QK[j]), ("dwc", ev_dwc)], inc="V")

                def emit_exp(j):
                    c0, c1 = max(0, j - 8) * 128, (min(3, j) + 1) * 128
                    w = [("V", ev_msk[j])] if j in ev_msk else [("T", ev_QK[j])]
                    if j >= 2:
                        w.append(("T", ev_l[j - 2]))
                    ev_exp[j] = p.op("S", lambda e, j=j, c0=c0, c1=c1: e.activation(
                        B(PS0 + (j % 2) * 512 + c0, c1 - c0), pb[j % 2][:, c0:c1], AF.Exp),
                        waits=w, inc="S")

                def emit_avl(j, g=g, kv=kv, ab=ab, lb=lb, hh=hh):
                    kt = 4 * g + j
                    c0, c1 = max(0, j - 8) * 128, (min(3, j) + 1) * 128
                    w_av = [("S", ev_exp[j])]
                    if j == 0 and (hh - 2) in ev_attnevac:
                        w_av.append(("V", ev_attnevac[hh - 2]))
                    p.op("T", lambda e, j=j, kt=kt, c0=c0, c1=c1, kv=kv, ab=ab: e.matmul(
                        pb[ab][:, c0:c1], B(VS + kt * 512 + kv * 128, 128),
                        B(PS0 + (j % 2) * 512 + c0, c1 - c0),
                        start=(j == 0), stop=(j == 11)), waits=w_av)
                    w_l = []
                    if j == 0:
                        if (hh - 2) in ev_ln:
                            w_l.append(("S", ev_ln[hh - 2]))
                        w_l.append(("dwc", ev_dwc))
                    ev_l[j] = p.op("T", lambda e, j=j, c0=c0, c1=c1, lb=lb: e.matmul(
                        pb[lb][:, c0:c1], B(ONES, 128),
                        B(PS0 + (j % 2) * 512 + c0, c1 - c0),
                        start=(j == 0), stop=(j == 11)), waits=w_l, inc="T")

                emit_qk(0)
                emit_qk(1)
                for j in range(12):
                    emit_exp(j)
                    emit_avl(j)
                    if j + 2 <= 11:
                        emit_qk(j + 2)
                ev_lastQK = ev_l[11]
                ev_phm = p.op("V", lambda e, lb=lb, g=g: e.tensor_tensor(
                    pb[lb][:, :], pb[lb][:, :], PH[:, g * 512:(g + 1) * 512], ALU.add),
                    waits=[("T", ev_l[11]), ("dwc", ev_dwc)], inc="V")
                ev_ln[hh] = p.op("S", lambda e, lb=lb: e.activation(
                    FAR[:, 0:512], pb[lb][:, :], AF.Ln), waits=[("V", ev_phm)], inc="S")
                w_rc = [("V", ev_attnevac[hh - 1])] if (hh - 1) in ev_attnevac else []
                ev_rcpb = p.op("S", lambda e: e.activation(
                    FAR[:, 512:1024], FAR[:, 0:512], AF.Exp, scale=-1.0),
                    waits=w_rc, inc="S")
                ev_attnevac[hh] = p.op("V", lambda e, g=g, h=h, ab=ab: e.tensor_tensor(
                    B(XT_OWN + g * 8192 + h * 512, 512), pb[ab][:, :],
                    FAR[:, 512:1024], ALU.mult), waits=[("S", ev_rcpb)], inc="V")
                if g == 0 and h % 2 == 1:
                    idx = h // 2
                    if idx < 4:
                        v_group(12 + idx, bank=6 + idx % 2, eng="V")
                    else:
                        k_group(idx - 4, 3, bank=6 + idx % 2, eng="V")
                    ev_late_kv[0] = p.cnt["V"]
                if g == 1 and h % 2 == 1:
                    idx = h // 2
                    o_group(idx // 4, 0, idx % 4, [6, 7], eng="V")
                if g == 1 and h % 2 == 1 and h // 2 + 1 < 8:
                    ln_stats_load(h // 2 + 1)
                if g == 0 and h == 15:
                    ln_stats_load(0)
                if g == 1 and h % 2 == 0:
                    rc = h // 2
                    ln_stats_chunk(d_xo, rc * 128, rc % 2, rc)

        sP3 = p.cnt["S"]
        bank_free[0] = bank_free[1] = ("S", sP3)
        bank_free[4] = bank_free[5] = ("S", sP3)
        bank_free[2] = ("V", ev_attnevac[30])
        bank_free[3] = ("V", ev_attnevac[31])

        def wo_late_load(q, ring):
            p.op(ring, lambda e, q=q: e.dma_start(
                B(R0 + (q % 2) * 8192, 8192).rearrange("p (b o) -> p b o", b=16),
                d_wob[:, (q - 2) * 512:(q - 1) * 512]
                .rearrange("(b p) o -> p b o", p=128)),
                waits=[("T", o_state[("q", q - 2)]), ("dcwo", ev_dcwo)],
                inc16=f"dwo{q}")

        tp_after_rc = {}
        ev_r = {}

        def tail_apply(rc):
            slot = rc % 2
            ro, no = 32 + rc * 2, 33 + rc * 2
            ev_ap2 = p.op("V", lambda e, sl=slot, ro=ro, no=no,
                          xo=B(XNT + slot * 2048, 2048): e.tensor_scalar(
                xo, XB[sl][:, :], ST[:, ro:ro + 1], ST[:, no:no + 1],
                ALU.mult, ALU.add),
                waits=[(f"dx{slot}", ev_r[rc]), xns_free[slot], ("S", sP3)],
                inc="V")
            xb_free[slot] = ("V", ev_ap2)
            tp_cnt[slot] += 1
            p.op("Y", lambda e, sl=slot, t=rc: e.dma_start(
                B(XT_P4, 16384).rearrange("p (b t) -> p b t", b=16)[:, :, t * 128:(t + 1) * 128],
                B(XNT + sl * 2048, 2048).rearrange("p (b h) -> p b h", b=16),
                transpose=True), waits=[("V", ev_ap2), ("T", ev_lastQK)],
                inc16=f"dtp{slot}")
            xns_free[slot] = (f"dtp{slot}", tp_cnt[slot] * 16)
            if rc in (3, 7):
                tp_after_rc[rc] = (tp_cnt[0] * 16, tp_cnt[1] * 16)

        ev_r[0] = load_x(d_xo, 0 * 128, 0, ring="S")
        ev_r[1] = load_x(d_xo, 1 * 128, 1, ring="S")
        tail_apply(0)
        tail_apply(1)
        ev_r[2] = load_x(d_xo, 2 * 128, 0, ring="S")
        ev_r[3] = load_x(d_xo, 3 * 128, 1, ring="S")
        for tl in range(4):
            o_group(0, 1, tl, [0, 1, 6, 7], eng="V")
        wo_late_load(2, "Y")
        tail_apply(2)
        tail_apply(3)
        ev_r[4] = load_x(d_xo, 4 * 128, 0, ring="S")
        for tl in range(4):
            o_group(1, 1, tl, [0, 1, 6, 7], eng="V")
        wo_late_load(3, "S")
        tail_apply(4)
        ev_r[5] = load_x(d_xo, 5 * 128, 1, ring="S")
        for tl in range(4):
            o_group(2, 0, tl, [0, 1, 6, 7], eng="V")
        tail_apply(5)
        ev_r[6] = load_x(d_xo, 6 * 128, 0, ring="S")
        for tl in range(4):
            o_group(2, 1, tl, [0, 1, 6, 7], eng="V")
        ev_r[7] = load_x(d_xo, 7 * 128, 1, ring="S")
        for tl in range(4):
            o_group(3, 0, tl, [0, 1, 6, 7], eng="V")
        tail_apply(6)
        for tl in range(4):
            o_group(3, 1, tl, [0, 1, 6, 7], eng="V")
        tail_apply(7)
        ev_O_T_all = o_state["last_T"]
        gu_slot_free = {i: None for i in range(4)}
        gu_cnt = [0, 0, 0, 0]
        ev_gu_last = None
        sg_evs = {}
        sg_free = {0: None, 1: None}
        gu_mm_ev = {}

        def gu_transpose(ic):
            for wt in range(2):
                slot = wt + 2 * (ic % 2)
                base = GUW[slot]
                srcw = d_gwb if wt == 0 else d_uwb
                w_tp = [(("dcg", ev_dcg) if wt == 0 else ("dcu", ev_dcu)),
                        gu_slot_free[slot]]
                if ic < 2 and slot < 2:
                    w_tp.append(("T", ev_O_T_all))
                if ic == 2:
                    w_tp += [("dgu2", p.cnt["dgu2"]), ("dgu3", p.cnt["dgu3"])]
                gu_cnt[slot] += 16
                p.op("Y" if ic < 2 else "S",
                     lambda e, b=base, s=srcw, r=ic * 128: e.dma_start(
                    B(b, 2048).rearrange("p (m q) -> p m q", m=16),
                    s[r:r + 128, :].rearrange("p (m q) -> p m q", m=16),
                    transpose=True), waits=w_tp, inc16=f"dgu{slot}")

        gu_transpose(0)
        gu_transpose(1)
        for rc in range(8):
            slot = rc % 2
            ev_x2 = load_x(d_xo, rc * 128, slot, ring="S")
            ev_xa = p.op("V", lambda e, rc=rc, sl=slot: e.tensor_tensor(
                B(AO + rc * 2048, 2048), B(AO + rc * 2048, 2048), XB[sl][:, :],
                ALU.add), waits=[(f"dx{slot}", ev_x2), ev_O_evac_tc[rc]], inc="V")
            xb_free[slot] = ("V", ev_xa)
        ev_tp0_all, ev_tp1_all = tp_cnt[0] * 16, tp_cnt[1] * 16

        for ic in range(NIC):
            wtev = {}
            for wt in range(2):
                slot = wt + 2 * (ic % 2)
                base = GUW[slot]
                evs = []
                for tb in range(2):
                    bank = wt * 2 + tb
                    t0, t1 = tp_after_rc[3 if tb == 0 else 7]
                    w0 = [(f"dgu{slot}", gu_cnt[slot]), ("dtp0", t0), ("dtp1", t1)]
                    tiles = [(B(base + hc * 128, 128),
                              B(XT_P4 + hc * 1024 + tb * 512, 512)) for hc in range(NHC)]
                    evt = mm_group(bank, 512, tiles, w0)
                    evs.append(evt)
                    wtev[(wt, tb)] = evt
                    ev_gu_last = evt
                if wt == 1:
                    gu_mm_ev[ic] = evs[-1]
                    gu_slot_free[0 + 2 * (ic % 2)] = ("T", evs[-1])
                    gu_slot_free[1 + 2 * (ic % 2)] = ("T", evs[-1])
            for tb in range(2):
                ev_si = evac("S", 0 + tb,
                    lambda e, b=0 + tb, c=ic: e.activation(
                        B(PS0 + (b % 2) * 512, 512), pb[b][:, :], AF.Silu,
                        scale=GSC[:, c:c + 1]),
                    waits=[("T", wtev[(0, tb)]), ("dwc", ev_dwc), sg_free[tb]])
                sg_evs[tb] = ("S", ev_si)
            for tb in range(2):
                ev_h = evac("V", 2 + tb,
                    lambda e, b=2 + tb, c=ic, tb=tb: e.scalar_tensor_tensor(
                        B(hdn_base(c) + tb * 512, 512), pb[b][:, :], USC[:, c:c + 1],
                        B(PS0 + tb * 512, 512), ALU.mult, ALU.mult),
                    waits=[("T", wtev[(1, tb)]), sg_evs[tb], ("dwc", ev_dwc)])
                sg_free[tb] = ("V", ev_h)
            if ic + 2 < NIC:
                gu_transpose(ic + 2)
            if stop_after == "p4ic2" and ic == 1:
                do_taps([("dbg_guw", GUW[0], 2048), ("dbg_guw1", GUW[1], 2048),
                         ("dbg_guw2", GUW[2], 2048), ("dbg_guw3", GUW[3], 2048),
                         ("dbg_h01", 0, 2048), ("dbg_xto3", XT_OWN, 16384)])
                break

        if stop_after == "p4gu":
            do_taps([("dbg_hdnA", 0, 16384), ("dbg_hdnB", 49152, 16384),
                     ("dbg_hdnC", 65536, 12288)])
        dw_slot_free = {0: None, 1: None}
        dw_cnt = [0, 0]
        ev_down_last_T = None
        for hb in range(8):
            dsl = hb % 2
            w_tp = [("dcd", ev_dcd), ("T", ev_gu_last), dw_slot_free[dsl]]
            for hcs in range(2):
                hc = hb * 2 + hcs
                if dsl == 0:
                    dw_cnt[dsl] += 16
                    p.op("S", lambda e, hc=hc, par=hcs: e.dma_start(
                        B(DWA, 11264).rearrange("p (m q) -> p m q", m=NIC)[:, :, par * 128:par * 128 + 128],
                        d_dwb[hc * 128:(hc + 1) * 128, :].rearrange("p (m q) -> p m q", m=NIC),
                        transpose=True), waits=(w_tp if hcs == 0 else ()), inc16=f"ddw{dsl}")
                else:
                    for pi, (pbase, iclo, nic) in enumerate(DWB_PIECES):
                        dw_cnt[dsl] += 16
                        p.op("S", lambda e, hc=hc, par=hcs, b=pbase, lo=iclo, n=nic:
                             e.dma_start(
                                 B(b, n * 256).rearrange("p (m q) -> p m q", m=n)[:, :, par * 128:par * 128 + 128],
                                 d_dwb[hc * 128:(hc + 1) * 128, lo * 128:(lo + n) * 128]
                                 .rearrange("p (m q) -> p m q", m=n),
                                 transpose=True),
                             waits=(w_tp if (hcs == 0 and pi == 0) else ()), inc16=f"ddw{dsl}")
            ev_ddw = dw_cnt[dsl]
            stage = XB[dsl]
            for tl in range(8):
                bank = 4 + (hb * 8 + tl) % 4
                if dsl == 0:
                    tiles = [(B(hdn_base(ic) + tl * 128, 128),
                              B(DWA + ic * 256, 256)) for ic in range(NIC)]
                else:
                    tiles = [(B(hdn_base(ic) + tl * 128, 128),
                              B(dwb_tile(ic), 256)) for ic in range(NIC)]
                evt = mm_group(bank, 256, tiles, [(f"ddw{dsl}", ev_ddw)])
                ev_down_last_T = evt
                dw_slot_free[dsl] = ("T", evt)
                p.op("V", lambda e, b=bank, st=stage, tl=tl, hb=hb: e.tensor_tensor(
                    st[:, tl * 256:(tl + 1) * 256], pb[b][:, 0:256],
                    B(DS + hb * 256, 256), ALU.mult),
                    waits=[("T", evt), ("dwc", ev_dwc), xb_free[dsl]])
                ev_e2 = evac("V", bank,
                    lambda e, st=stage, tl=tl, hb=hb: e.tensor_tensor(
                        st[:, tl * 256:(tl + 1) * 256], st[:, tl * 256:(tl + 1) * 256],
                        B(AO + tl * 2048 + hb * 256, 256), ALU.add),
                    waits=[])
                p.op("Y", lambda e, st=stage, tl=tl, hb=hb: e.dma_start(
                    d_out[tl * 128:(tl + 1) * 128, hb * 256:(hb + 1) * 256],
                    st[:, tl * 256:(tl + 1) * 256]),
                    waits=[("V", ev_e2)], inc16="dout")
            xb_free[dsl] = ("dout", p.cnt["dout"])

        p.op("Y", lambda e: e.wait_ge(sems["dout"], p.cnt["dout"]))

        with nc.Block() as blk:
            def mk_body(lst):
                def body(eng):
                    for fn, w, inc, inc16 in lst:
                        for s, v in w:
                            eng.wait_ge(sems[s], v)
                        inst = fn(eng)
                        if inc:
                            inst.then_inc(sems[inc], 1)
                        if inc16:
                            inst.then_inc(sems[inc16], 16)
                return body
            blk.tensor(mk_body(p.ops["T"]))
            blk.vector(mk_body(p.ops["V"]))
            blk.scalar(mk_body(p.ops["S"]))
            blk.sync(mk_body(p.ops["Y"]))
            blk.gpsimd(mk_body(p.ops["G"]))
    return nc


def kernel(**inputs):
    f32 = np.float32
    x = np.ascontiguousarray(np.asarray(inputs["L0_attn_hidden_states"]).reshape(8192, 2048))
    wq = np.ascontiguousarray(inputs["L0_attn_Wq"])
    wk = np.ascontiguousarray(inputs["L0_attn_Wk"])
    wv = np.ascontiguousarray(inputs["L0_attn_Wv"])
    wo = np.ascontiguousarray(inputs["L0_attn_Wo"])
    gw = np.ascontiguousarray(inputs["L0_ffn_gate_w"])
    uw = np.ascontiguousarray(inputs["L0_ffn_up_w"])
    dw = np.ascontiguousarray(inputs["L0_ffn_down_w"])
    gs = np.ascontiguousarray(np.asarray(inputs["L0_ffn_gate_s"]).reshape(NIC, 128).T)
    us = np.ascontiguousarray(np.asarray(inputs["L0_ffn_up_s"]).reshape(NIC, 128).T)
    ds = np.ascontiguousarray(np.broadcast_to(
        np.asarray(inputs["L0_ffn_down_s"])[None, :], (128, 2048)))
    kr = np.arange(128)[:, None]
    qc = np.arange(128)[None, :]
    msk = np.ascontiguousarray(np.concatenate([
        np.where(qc <= kr, 0.0, -1e30),
        np.where(kr <= qc, 0.0, -1e30),
    ], axis=1).astype(f32))
    ones = np.ones((128, 128), f32)
    zeros_ph = np.zeros((128, 1024), f32)
    ph0 = np.ascontiguousarray(np.broadcast_to(
        (np.arange(1024) - 1024.0).astype(f32)[None, :], (128, 1024)))

    nc = build_program()
    in_maps = []
    for c in range(8):
        x_own = np.ascontiguousarray(x[c * TOK:(c + 1) * TOK])
        x_prev = (np.ascontiguousarray(x[(c - 1) * TOK:c * TOK]) if c > 0
                  else np.zeros((TOK, H), f32))
        in_maps.append(dict(
            x_own=x_own, x_prev=x_prev,
            wq=wq, wk=wk, wv=wv, wo=wo, gate_w=gw, up_w=uw, down_w=dw,
            gate_s=gs, up_s=us, down_s_rep=ds, masks=msk,
            phantom=(ph0 if c == 0 else zeros_ph), ones_in=ones))
    res = run_bass_kernel_spmd(nc, in_maps, core_ids=list(range(8)))
    out = np.concatenate([r["out"] for r in res.results], axis=0)
    return out.reshape(1, 8192, 2048).astype(np.float32, copy=False)
